# version 15
# baseline (speedup 1.0000x reference)
import numpy as np
import ml_dtypes
from contextlib import ExitStack

import concourse.bass as bass
import concourse.mybir as mybir
from concourse.bass_utils import run_bass_kernel_spmd

F32 = mybir.dt.float32
BF16 = mybir.dt.bfloat16
AF = mybir.ActivationFunctionType
ALU = mybir.AluOpType

D = 1024
DC = 8
TPC = 2048
NQT = 4
QT = 512
NKT = 32
DFF = 4096
MAGIC = float(np.float32(3 * 2 ** 22))
CLIP = float(np.float32(1.0 - 1e-6))
INV1024 = float(np.float32(1.0 / 1024.0))
EPS5 = 1e-5
EPS12 = 1e-12
EPS24 = 1e-24
PI = float(np.pi)

ENGINES = ("sync", "tensor", "vector", "scalar", "gpsimd")

SAFE_EDGES = False
TAPS = False
PHASES = 6
A_QT = 4
A_STAGE = 5


class Sem:
    def __init__(self, h):
        self.h = h
        self.n = 0


class Prog:
    def __init__(self):
        self.steps = {e: [] for e in ENGINES}
        self.cur = {e: {} for e in ENGINES}

    def on(self, eng, fn, waits=(), inc=None, inc_by=1):
        ss = getattr(self, "selfsems", {}).get(eng) if SAFE_EDGES else None
        if ss is not None:
            waits = list(waits) + [(ss, ss.n)]
            if inc is None:
                inc = ss
        wl = []
        cur = self.cur[eng]
        for s, v in waits:
            if v <= 0:
                continue
            k = id(s)
            if cur.get(k, -1) < v:
                cur[k] = v
                wl.append((s.h, v))
        if inc is not None:
            inc.n += inc_by

        def run(e, wl=wl, fn=fn, inc=inc, inc_by=inc_by):
            for h, v in wl:
                e.wait_ge(h, v)
            ins = fn(e)
            if inc is not None:
                ins.then_inc(inc.h, inc_by)
        self.steps[eng].append(run)


def build_program():
    nc = bass.Bass("TRN2", target_bir_lowering=False)

    def din(name, shape):
        return nc.dram_tensor(name, shape, F32, kind="ExternalInput").ap()

    d_xT = din("xT", [D, TPC])
    d_knT = din("knTw", [D, DFF])
    d_qw = din("qw", [DFF, D])
    d_wqkv = din("wqkv", [D, 3 * D])
    d_wout = din("wout", [D, D])
    d_w1T = din("w1T", [D, DFF])
    d_w2T = din("w2T", [DFF, D])
    d_lnw = din("lnw_p", [128, DC])
    d_lnb = din("lnb_p", [128, DC])
    d_aab = din("aab_p", [128, DC])
    d_bqkv = din("bqkv_p", [128, 16])
    d_bv = din("bv_row", [1, D])
    d_b1 = din("b1_p", [128, NKT])
    d_b2 = din("b2_p", [128, DC])
    d_bout = din("bout_row", [1, D])
    d_out = nc.dram_tensor("outT", [D, TPC], F32, kind="ExternalOutput").ap()

    _wk = "ExternalOutput" if TAPS else "Internal"
    s_qwbf = nc.dram_tensor("qw_bf", [DFF, D], BF16, kind=_wk).ap()
    s_wqkvbf = nc.dram_tensor("wqkv_bf", [D, 3 * D], BF16, kind=_wk).ap()
    s_woutbf = nc.dram_tensor("wout_bf", [D, D], BF16, kind=_wk).ap()
    s_knbf = nc.dram_tensor("knT_bf", [D, DFF], BF16, kind=_wk).ap()
    s_wq1 = nc.dram_tensor("wq1T_bf", [D, DFF], BF16, kind=_wk).ap()
    s_wq2 = nc.dram_tensor("wq2T_bf", [DFF, D], BF16, kind=_wk).ap()
    cc_inK = nc.dram_tensor("cc_inK", [D, TPC], BF16)
    cc_outK = nc.dram_tensor("cc_outK", [2 * D, TPC], BF16)
    cc_inV = nc.dram_tensor("cc_inV", [TPC, D], BF16)
    cc_outV = nc.dram_tensor("cc_outV", [2 * TPC, D], BF16)
    d_taps = [nc.dram_tensor(f"tap{i}", [D, TPC], F32, kind="ExternalOutput").ap()
              for i in range(3)] if TAPS else None
    d_tapQ = nc.dram_tensor("tapQ", [128, 4096], BF16,
                            kind="ExternalOutput").ap() if TAPS else None
    d_tapB = nc.dram_tensor("tapB", [128, 16384], BF16,
                            kind="ExternalOutput").ap() if TAPS else None
    d_tapX = nc.dram_tensor("tapX", [128, 4096], BF16,
                            kind="ExternalOutput").ap() if TAPS else None
    d_tapG = nc.dram_tensor("tapG", [128, 8], F32,
                            kind="ExternalOutput").ap() if TAPS else None

    P = Prog()
    on = P.on

    with ExitStack() as ctx:
        E = ctx.enter_context

        xT = E(nc.sbuf_tensor("s_xT", [128, DC, TPC], F32))
        AR32 = E(nc.sbuf_tensor("AR32", [128, 8 * 1024], F32))
        ARB = E(nc.sbuf_tensor("ARB", [128, 45 * 1024], BF16))
        ones_col = E(nc.sbuf_tensor("ones_col", [128, 1], BF16))
        ones_colf = E(nc.sbuf_tensor("ones_colf", [128, 1], F32))
        ones_rowD = E(nc.sbuf_tensor("ones_rowD", [1, 128], F32))
        ones_rowN = E(nc.sbuf_tensor("ones_rowN", [1, QT], F32))
        p_lnw = E(nc.sbuf_tensor("p_lnw", [128, DC], F32))
        p_lnb = E(nc.sbuf_tensor("p_lnb", [128, DC], F32))
        p_aab = E(nc.sbuf_tensor("p_aab", [128, DC], F32))
        p_bqkv = E(nc.sbuf_tensor("p_bqkv", [128, 16], F32))
        p_bv = E(nc.sbuf_tensor("p_bv", [1, D], F32))
        p_b1 = E(nc.sbuf_tensor("p_b1", [128, NKT], F32))
        p_b2 = E(nc.sbuf_tensor("p_b2", [128, DC], F32))
        p_bout = E(nc.sbuf_tensor("p_bout", [1, D], F32))
        p_gsc = E(nc.sbuf_tensor("p_gsc", [128, 2], F32))
        r_mu = E(nc.sbuf_tensor("r_mu", [1, QT], F32))
        r_t1 = E(nc.sbuf_tensor("r_t1", [1, QT], F32))
        r_t2 = E(nc.sbuf_tensor("r_t2", [1, QT], F32))
        r_rstd = E(nc.sbuf_tensor("r_rstd", [1, QT], F32))
        r_nmr = E(nc.sbuf_tensor("r_nmr", [1, QT], F32))
        rx0 = r_t1
        rx1 = r_t2
        r_g = E(nc.sbuf_tensor("r_g", [1, 4], F32))
        c_acc = E(nc.sbuf_tensor("c_acc", [128, 32], F32))
        c_acc1 = E(nc.sbuf_tensor("c_acc1", [128, 1], F32))
        c_half = E(nc.sbuf_tensor("c_half", [128, 1], F32))
        c_eps5 = E(nc.sbuf_tensor("c_eps5", [1, 1], F32))
        c_eps12 = E(nc.sbuf_tensor("c_eps12", [1, 1], F32))

        PS4 = E(nc.psum_tensor("PS4", [128, 2048], F32))
        B4 = [E(nc.psum_tensor(f"B4_{i}", [128, 512], F32)) for i in range(4)]

        tE = Sem(E(nc.semaphore("tE")))
        vE = Sem(E(nc.semaphore("vE")))
        sE = Sem(E(nc.semaphore("sE")))
        gE = Sem(E(nc.semaphore("gE")))
        ccK = Sem(E(nc.semaphore("ccK")))
        ccV = Sem(E(nc.semaphore("ccV")))
        P.selfsems = {"tensor": tE, "vector": vE, "scalar": sE, "gpsimd": gE}

        def mksem(name):
            return Sem(E(nc.semaphore(name)))
        ldXs = mksem("ldXs")
        p0L = [mksem("p0L0"), mksem("p0L1"), mksem("p0L2")]
        p0S = [mksem("p0S0"), mksem("p0S1"), mksem("p0S2")]
        knLs = [mksem("knL0"), mksem("knL1")]
        qwLs = [mksem("qwL0"), mksem("qwL1")]
        w1Ls = [mksem("w1L0"), mksem("w1L1")]
        w2Ls = [mksem("w2L0"), mksem("w2L1")]
        wqLs = [mksem("wqL0"), mksem("wqL1")]
        wvLs = mksem("wvLs")
        ktLs = [mksem("ktL0"), mksem("ktL1")]
        opLs = [mksem("opL0"), mksem("opL1")]
        vxLs = [mksem("vxL0"), mksem("vxL1")]
        ksSs = mksem("ksSs")
        vpSs = [mksem("vpS0"), mksem("vpS1")]
        outQ = mksem("outQ")

        F0 = AR32[:, 0:4096]
        F2 = AR32[:, 4096:8192]
        F0v = F0.rearrange("p (g n) -> p g n", n=QT)
        F2v = F2.rearrange("p (g n) -> p g n", n=QT)
        TMPv = F2v

        def av(lo, hi, inner=None):
            a = ARB[:, lo:hi]
            if inner is not None:
                a = a.rearrange("p (a n) -> p a n", n=inner)
            return a
        K = 1024
        xbf = av(0, 4 * K, QT)
        xsq = av(4 * K, 8 * K, QT)
        Q1 = xsq
        zb = av(8 * K, 12 * K, QT)
        z2b = av(12 * K, 16 * K, QT)
        sgb = av(16 * K, 20 * K, QT)
        sgf = av(16 * K, 20 * K)
        Bbuf = av(20 * K, 36 * K, QT)
        knb = [av(36 * K + i * 4 * K, 36 * K + (i + 1) * 4 * K, QT)
               for i in range(2)]
        qwb = [av(44 * K + i * 512, 44 * K + (i + 1) * 512)
               for i in range(2)]
        w1s = [av(16 * K + i * 2 * K, 16 * K + (i + 1) * 2 * K, 256)
               for i in range(2)]
        hbuf = av(20 * K, 36 * K, QT)
        w2s = [av(36 * K + i * 4 * K, 36 * K + (i + 1) * 4 * K, 128)
               for i in range(2)]
        wqs = [av(16 * K + i * K, 16 * K + (i + 1) * K, 128)
               for i in range(2)]
        vplain = [av(12 * K + i * K, 12 * K + (i + 1) * K) for i in range(2)]
        kstg = av(14 * K, 14 * K + 512)
        QTb = av(18 * K, 34 * K, TPC)
        wv = av(36 * K, 44 * K, D)
        Pbuf = [av(i * K, (i + 1) * K, QT) for i in range(2)]
        Obstg = av(2 * K, 2 * K + 512)
        wobs = [av(2560 + i * K, 2560 + (i + 1) * K, 128)
                for i in range(2)]
        attno = av(4608, 4608 + 4 * K, QT)
        KTs = [av(10 * K + i * 4 * K, 10 * K + (i + 1) * 4 * K)
               for i in range(2)]
        Vxs = [av(34 * K + i * 4096, 34 * K + (i + 1) * 4096, 128)
               for i in range(2)]
        p0s = [av(0, 4 * K), av(4 * K, 8 * K), av(20 * K, 24 * K)]

        for c in range(DC):
            on("sync", lambda e, c=c: e.dma_start(
                out=xT[:, c, :], in_=d_xT[c * 128:(c + 1) * 128, :]),
                inc=ldXs, inc_by=16)
        for dst, srcp in [(p_lnw, d_lnw), (p_lnb, d_lnb), (p_aab, d_aab),
                          (p_bqkv, d_bqkv), (p_bv, d_bv), (p_b1, d_b1),
                          (p_b2, d_b2), (p_bout, d_bout)]:
            on("sync", lambda e, dst=dst, srcp=srcp: e.dma_start(
                out=dst[:, :], in_=srcp[:, :]), inc=ldXs, inc_by=16)
        X_LOADED = ldXs.n

        on("gpsimd", lambda e: e.memset(ones_col[:, :], 1.0), inc=gE)
        on("gpsimd", lambda e: e.memset(ones_colf[:, :], 1.0), inc=gE)
        on("gpsimd", lambda e: e.memset(ones_rowD[:, :], 1.0), inc=gE)
        on("gpsimd", lambda e: e.memset(ones_rowN[:, :], 1.0), inc=gE)
        on("gpsimd", lambda e: e.memset(c_half[:, :], 0.5), inc=gE)
        on("gpsimd", lambda e: e.memset(c_eps5[:, :], EPS5), inc=gE)
        on("gpsimd", lambda e: e.memset(c_eps12[:, :], EPS12), inc=gE)
        GINIT = gE.n

        knTv = d_knT.rearrange("(c p) i -> p c i", p=128)
        knbfv = s_knbf.rearrange("(c p) i -> p c i", p=128)
        p0sv = [p0s[i].rearrange("p (a n) -> p a n", n=QT) for i in range(2)]
        bF = dict(v=0, s=0, st=0)
        aS = [dict(v=0, s=0, st=0, t=0), dict(v=0, s=0, st=0, t=0)]
        a_slot = dict(i=0)
        AR32b = ARB[:, 24 * K:32 * K].bitcast(F32)
        aFbuf = [F0, AR32b]
        a_stg = [0, 2]
        stepsA, stepsB = [], []

        def mk_cast(sap, dap, cols):
            def run():
                on("sync", lambda e: e.dma_start(out=F2[:, 0:cols], in_=sap),
                   waits=[(vE, bF["v"]), (sE, bF["s"]), (p0S[1], bF["st"])],
                   inc=p0L[1], inc_by=16)
                ld = (p0L[1], p0L[1].n)
                on("vector", lambda e: e.tensor_copy(
                    p0s[1][:, 0:cols], F2[:, 0:cols]), waits=[ld], inc=vE)
                bF["v"] = vE.n
                on("sync", lambda e: e.dma_start(out=dap, in_=p0s[1][:, 0:cols]),
                   waits=[(vE, vE.n)], inc=p0S[1], inc_by=16)
                bF["st"] = p0S[1].n
            return run

        for srcp, dstp, rows, cols in [(d_qw, s_qwbf, DFF, D),
                                       (d_wqkv, s_wqkvbf, D, 3 * D),
                                       (d_wout, s_woutbf, D, D)]:
            nc_ = rows // 128
            gsz = max(1, 4096 // cols)
            for c0 in range(0, nc_, gsz):
                g = min(gsz, nc_ - c0)
                sap = srcp.rearrange("(c p) f -> p c f", p=128)[:, c0:c0 + g, :]
                dap = dstp.rearrange("(c p) f -> p c f", p=128)[:, c0:c0 + g, :]
                stepsB.append(mk_cast(sap, dap, g * cols))

        def mk_kn(ib):
            def run():
                on("sync", lambda e: e.dma_start(
                    out=F2v[:, :, :], in_=knTv[:, :, ib * QT:(ib + 1) * QT]),
                    waits=[(vE, bF["v"]), (sE, bF["s"]), (p0S[1], bF["st"])],
                    inc=p0L[1], inc_by=16)
                ld = (p0L[1], p0L[1].n)
                on("scalar", lambda e: e.activation(
                    p0sv[1][:, :, :], F2v[:, :, :], AF.Square),
                    waits=[ld], inc=sE)
                sq = sE.n

                def knmm(e):
                    last = None
                    for c in range(DC):
                        last = e.matmul(B4[0][0:1, :], ones_col[:, :],
                                        p0sv[1][:, c, :],
                                        start=(c == 0), stop=(c == DC - 1))
                    return last
                on("tensor", knmm, waits=[(sE, sq), (gE, GINIT)], inc=tE)
                on("vector", lambda e: e.tensor_scalar(
                    r_t1[:, :], B4[0][0:1, :], EPS24, None, ALU.max),
                    waits=[(tE, tE.n)], inc=vE)
                on("scalar", lambda e: e.activation(r_t2[:, :], r_t1[:, :],
                                                    AF.Ln),
                   waits=[(vE, vE.n)], inc=sE)
                on("scalar", lambda e: e.activation(r_rstd[:, :], r_t2[:, :],
                                                    AF.Exp, scale=-0.5), inc=sE)
                on("tensor", lambda e: e.matmul(
                    B4[1][:, :], ones_rowD[0:1, :], r_rstd[0:1, :],
                    start=True, stop=True), waits=[(sE, sE.n)], inc=tE)
                bcv = tE.n

                def knscale(e):
                    last = None
                    for c in range(DC):
                        last = e.tensor_tensor(p0sv[1][:, c, :], F2v[:, c, :],
                                               B4[1][:, :], ALU.mult)
                    return last
                on("vector", knscale, waits=[(tE, bcv)], inc=vE)
                bF["v"] = vE.n
                bF["s"] = sE.n
                on("sync", lambda e: e.dma_start(
                    out=knbfv[:, :, ib * QT:(ib + 1) * QT],
                    in_=p0sv[1][:, :, :]),
                    waits=[(vE, vE.n)], inc=p0S[1], inc_by=16)
                bF["st"] = p0S[1].n
            return run

        for ib in range(DC):
            stepsB.append(mk_kn(ib))

        def mk_gamma(srcW, c0, g, cols, first, last, gcol):
            w = g * cols
            sap = srcW.rearrange("(c p) f -> p c f", p=128)[:, c0:c0 + g, :]

            def run():
                sl = a_slot["i"]; a_slot["i"] ^= 1
                aF = aS[sl]; fb = aFbuf[sl]
                stg = p0s[a_stg[sl]]; sg = a_stg[sl]
                on("scalar", lambda e: e.dma_start(out=fb[:, 0:w], in_=sap),
                    waits=[(vE, aF["v"]), (sE, aF["s"]), (p0S[sg], aF["st"])],
                    inc=p0L[sg], inc_by=16)
                ld = (p0L[sg], p0L[sg].n)
                on("scalar", lambda e: e.activation(
                    stg[:, 0:w], fb[:, 0:w], AF.Abs),
                    waits=[ld, (tE, aF["t"])], inc=sE)
                aF["s"] = sE.n

                def gmm(e):
                    last_i = None
                    nj = w // 128
                    for j in range(nj):
                        last_i = e.matmul(
                            gcol, stg[:, j * 128:(j + 1) * 128],
                            ones_col[:, :],
                            start=(first and j == 0),
                            stop=(last and j == nj - 1))
                    return last_i
                on("tensor", gmm, waits=[(sE, aF["s"]), (gE, GINIT)], inc=tE)
                aF["t"] = tE.n
            return run

        def mk_gfin(wi, rows, cols, gcol):
            rg0 = r_g[0:1, 2 * wi:2 * wi + 1]
            rg1 = r_g[0:1, 2 * wi + 1:2 * wi + 2]
            nelem = float(rows * cols)

            def run():
                on("vector", lambda e: e.tensor_copy(c_acc1[:, :], gcol),
                   waits=[(tE, max(aS[0]["t"], aS[1]["t"]))], inc=vE)
                on("tensor", lambda e: e.matmul(
                    B4[2][0:1, 1:2], c_acc1[:, :], ones_colf[:, :],
                    start=True, stop=True), waits=[(vE, vE.n)], inc=tE)
                on("vector", lambda e: e.tensor_scalar(
                    rg0, B4[2][0:1, 1:2], 1.0 / nelem, EPS5,
                    ALU.mult, ALU.add), waits=[(tE, tE.n)], inc=vE)
                on("vector", lambda e: e.reciprocal(rg1, rg0),
                   waits=[(vE, vE.n)], inc=vE)
                on("tensor", lambda e: e.matmul(
                    B4[2][:, 2:3], ones_rowD[0:1, :], rg1,
                    start=True, stop=True), waits=[(vE, vE.n)], inc=tE)
                on("vector", lambda e: e.tensor_copy(
                    p_gsc[:, wi:wi + 1], B4[2][:, 2:3]),
                   waits=[(tE, tE.n)], inc=vE)
                aS[0]["v"] = vE.n
                aS[1]["v"] = vE.n
            return run

        def mk_quant(srcW, dstW, c0, g, cols, wi):
            w = g * cols
            sap = srcW.rearrange("(c p) f -> p c f", p=128)[:, c0:c0 + g, :]
            dap = dstW.rearrange("(c p) f -> p c f", p=128)[:, c0:c0 + g, :]

            def run():
                sl = a_slot["i"]; a_slot["i"] ^= 1
                aF = aS[sl]; fb = aFbuf[sl]
                stg = p0s[a_stg[sl]]; sg = a_stg[sl]
                on("scalar", lambda e: e.dma_start(out=fb[:, 0:w], in_=sap),
                    waits=[(vE, aF["v"]), (sE, aF["s"]), (p0S[sg], aF["st"])],
                    inc=p0L[sg], inc_by=16)
                ld = (p0L[sg], p0L[sg].n)
                on("vector", lambda e: e.tensor_scalar(
                    fb[:, 0:w], fb[:, 0:w],
                    p_gsc[:, wi:wi + 1], MAGIC, ALU.mult, ALU.add),
                    waits=[ld], inc=vE)
                on("vector", lambda e: e.tensor_scalar(
                    fb[:, 0:w], fb[:, 0:w],
                    MAGIC, 1.0, ALU.subtract, ALU.min), inc=vE)
                on("vector", lambda e: e.tensor_scalar(
                    stg[:, 0:w], fb[:, 0:w],
                    -1.0, None, ALU.max), inc=vE)
                aF["v"] = vE.n
                on("scalar", lambda e: e.dma_start(out=dap, in_=stg[:, 0:w]),
                    waits=[(vE, vE.n)], inc=p0S[sg], inc_by=16)
                aF["st"] = p0S[sg].n
            return run

        for wi, (srcW, dstW, rows, cols) in enumerate(
                [(d_w1T, s_wq1, D, DFF), (d_w2T, s_wq2, DFF, D)]):
            nc_ = rows // 128
            gsz = max(1, 4096 // cols)
            units = list(range(0, nc_, gsz))
            gcol = B4[2][:, 0:1] if wi == 0 else B4[3][:, 0:1]
            for ui, c0 in enumerate(units):
                g = min(gsz, nc_ - c0)
                stepsA.append(mk_gamma(srcW, c0, g, cols,
                                       ui == 0, ui == len(units) - 1, gcol))
            stepsA.append(mk_gfin(wi, rows, cols, gcol))
            for c0 in units:
                g = min(gsz, nc_ - c0)
                stepsA.append(mk_quant(srcW, dstW, c0, g, cols, wi))

        ia = ib_ = 0
        while ia < len(stepsA) or ib_ < len(stepsB):
            if ia < len(stepsA):
                stepsA[ia](); ia += 1
            if ib_ < len(stepsB):
                stepsB[ib_](); ib_ += 1

        P0DONE = [(p0S[0], p0S[0].n), (p0S[1], p0S[1].n), (p0S[2], p0S[2].n)]

        def emit_ln(qt, w_prev=None, zout=None):
            if zout is None:
                zout = zb
            sl_ = slice(qt * QT, (qt + 1) * QT)
            wts = [(tE, tE.n), (sE, sE.n)] if w_prev is None else w_prev
            on("vector", lambda e: e.tensor_copy(xbf[:, :, :], xT[:, :, sl_]),
               waits=wts + [(vE, vE.n)], inc=vE)
            on("scalar", lambda e: e.activation(xsq[:, :, :], xT[:, :, sl_],
                                                AF.Square),
               waits=wts + [(vE, vE.n)], inc=sE)
            vcast, ssq = vE.n, sE.n

            def statmm(e):
                last = None
                for c in range(DC):
                    last = e.matmul(B4[0][0:1, :], ones_col[:, :], xbf[:, c, :],
                                    start=(c == 0), stop=(c == DC - 1))
                for c in range(DC):
                    last = e.matmul(B4[1][0:1, :], ones_col[:, :], xsq[:, c, :],
                                    start=(c == 0), stop=(c == DC - 1))
                return last
            on("tensor", statmm, waits=[(vE, vcast), (sE, ssq)], inc=tE)
            mmv = tE.n
            on("vector", lambda e: e.tensor_scalar(
                r_mu[:, :], B4[0][0:1, :], INV1024, None, ALU.mult),
                waits=[(tE, mmv)], inc=vE)
            on("vector", lambda e: e.scalar_tensor_tensor(
                r_t1[:, :], r_mu[:, :], -1.0, r_mu[:, :], ALU.mult, ALU.mult),
                inc=vE)
            on("vector", lambda e: e.scalar_tensor_tensor(
                r_t2[:, :], B4[1][0:1, :], INV1024, r_t1[:, :], ALU.mult, ALU.add),
                inc=vE)
            on("scalar", lambda e: e.activation(r_t1[:, :], r_t2[:, :], AF.Ln,
                                                bias=c_eps5[:, :]),
               waits=[(vE, vE.n)], inc=sE)
            on("scalar", lambda e: e.activation(r_rstd[:, :], r_t1[:, :], AF.Exp,
                                                scale=-0.5), inc=sE)
            srstd = sE.n
            on("vector", lambda e: e.scalar_tensor_tensor(
                r_nmr[:, :], r_mu[:, :], -1.0, r_rstd[:, :], ALU.mult, ALU.mult),
                waits=[(sE, srstd)], inc=vE)
            vnmr = vE.n
            on("tensor", lambda e: e.matmul(B4[2][:, :], ones_rowD[0:1, :],
                                            r_rstd[0:1, :], start=True, stop=True),
               waits=[(sE, srstd)], inc=tE)
            on("tensor", lambda e: e.matmul(B4[3][:, :], ones_rowD[0:1, :],
                                            r_nmr[0:1, :], start=True, stop=True),
               waits=[(vE, vnmr)], inc=tE)
            bcv = tE.n

            def applyA(e, sl_=sl_):
                last = None
                for c in range(DC):
                    last = e.tensor_tensor(TMPv[:, c, :], xT[:, c, sl_],
                                           B4[2][:, :], ALU.mult)
                return last

            def applyB(e, zout=zout):
                last = None
                for c in range(DC):
                    last = e.tensor_tensor(zout[:, c, :], TMPv[:, c, :],
                                           B4[3][:, :], ALU.add)
                return last
            on("vector", applyA, waits=[(tE, bcv)], inc=vE)
            on("vector", applyB, inc=vE)
            return vE.n

        def emit_l2recip(row_ap, wait_t):
            on("vector", lambda e: e.tensor_scalar(
                r_t1[:, :], row_ap, EPS24, None, ALU.max),
                waits=[(tE, wait_t)], inc=vE)
            on("scalar", lambda e: e.activation(r_t2[:, :], r_t1[:, :], AF.Ln),
               waits=[(vE, vE.n)], inc=sE)
            on("scalar", lambda e: e.activation(r_rstd[:, :], r_t2[:, :], AF.Exp,
                                                scale=-0.5), inc=sE)
            return sE.n

        tapsem2 = Sem(E(nc.semaphore("tapsem2"))) if TAPS else None

        def tap_sbuf(dst, src_ap):
            if not TAPS:
                return
            on("gpsimd", lambda e: e.dma_start(out=dst, in_=src_ap),
               waits=[(vE, vE.n), (sE, sE.n), (tE, tE.n)], inc=tapsem2,
               inc_by=16)
            on("vector", lambda e: e.engine_nop(),
               waits=[(tapsem2, tapsem2.n)], inc=vE)

        sb4 = [PS4[:, i * 512:(i + 1) * 512] for i in range(4)]
        evac_hist = []
        kn_ld_hist = []
        qw_hist = []
        def attna_pre(qt):
            wprev = None
            if qt == 0:
                wprev = [(ldXs, X_LOADED), (tE, tE.n), (sE, sE.n)] + P0DONE
            zready = emit_ln(qt, wprev)
            if A_STAGE < 2:
                return (vE.n, vE.n)

            on("scalar", lambda e: e.activation(xsq[:, :, :], zb[:, :, :],
                                                AF.Square),
               waits=[(vE, zready)], inc=sE)
            ssq = sE.n

            def statmm2(e):
                last = None
                for c in range(DC):
                    last = e.matmul(B4[0][0:1, :], ones_col[:, :], zb[:, c, :],
                                    start=(c == 0), stop=(c == DC - 1))
                for c in range(DC):
                    last = e.matmul(B4[1][0:1, :], ones_col[:, :], xsq[:, c, :],
                                    start=(c == 0), stop=(c == DC - 1))
                return last
            on("tensor", statmm2, waits=[(vE, zready), (sE, ssq)], inc=tE)
            mmv = tE.n
            on("vector", lambda e: e.tensor_scalar(
                r_mu[:, :], B4[0][0:1, :], INV1024, None, ALU.mult),
                waits=[(tE, mmv)], inc=vE)
            on("vector", lambda e: e.scalar_tensor_tensor(
                r_t1[:, :], r_mu[:, :], -1.0, r_mu[:, :], ALU.mult, ALU.mult),
                inc=vE)
            on("vector", lambda e: e.scalar_tensor_tensor(
                r_t2[:, :], B4[1][0:1, :], INV1024, r_t1[:, :], ALU.mult, ALU.add),
                inc=vE)
            on("scalar", lambda e: e.activation(r_t1[:, :], r_t2[:, :], AF.Ln,
                                                bias=c_eps12[:, :]),
               waits=[(vE, vE.n)], inc=sE)
            on("scalar", lambda e: e.activation(r_rstd[:, :], r_t1[:, :], AF.Exp,
                                                scale=-0.5), inc=sE)
            srstd = sE.n
            on("vector", lambda e: e.scalar_tensor_tensor(
                r_nmr[:, :], r_mu[:, :], -1.0, r_rstd[:, :], ALU.mult, ALU.mult),
                waits=[(sE, srstd)], inc=vE)
            vnmr = vE.n
            on("tensor", lambda e: e.matmul(B4[2][:, :], ones_rowD[0:1, :],
                                            r_rstd[0:1, :], start=True, stop=True),
               waits=[(sE, srstd)], inc=tE)
            on("tensor", lambda e: e.matmul(B4[3][:, :], ones_rowD[0:1, :],
                                            r_nmr[0:1, :], start=True, stop=True),
               waits=[(vE, vnmr)], inc=tE)
            bcv = tE.n

            def apply2a(e):
                last = None
                for c in range(DC):
                    last = e.scalar_tensor_tensor(
                        TMPv[:, c, :], zb[:, c, :], p_lnw[:, c:c + 1],
                        B4[2][:, :], ALU.mult, ALU.mult)
                return last

            def apply2b2(e):
                last = None
                for c in range(DC):
                    last = e.tensor_scalar(
                        F0v[:, c, :], B4[3][:, :], p_lnw[:, c:c + 1],
                        p_lnb[:, c:c + 1], ALU.mult, ALU.add)
                return last

            def apply2c(e):
                last = None
                for c in range(DC):
                    last = e.tensor_tensor(z2b[:, c, :], TMPv[:, c, :],
                                           F0v[:, c, :], ALU.add)
                return last
            on("vector", apply2a, waits=[(tE, bcv)], inc=vE)
            on("vector", apply2b2, inc=vE)
            on("vector", apply2c, inc=vE)
            z2ready = vE.n

            on("scalar", lambda e: e.activation(xsq[:, :, :], z2b[:, :, :],
                                                AF.Square),
               waits=[(vE, z2ready)], inc=sE)
            ssq = sE.n

            def statmm3(e):
                last = None
                for c in range(DC):
                    last = e.matmul(B4[0][0:1, :], ones_col[:, :], xsq[:, c, :],
                                    start=(c == 0), stop=(c == DC - 1))
                return last
            on("tensor", statmm3, waits=[(sE, ssq)], inc=tE)
            sr = emit_l2recip(B4[0][0:1, :], tE.n)
            on("tensor", lambda e: e.matmul(B4[1][:, :], ones_rowD[0:1, :],
                                            r_rstd[0:1, :], start=True, stop=True),
               waits=[(sE, sr)], inc=tE)
            bcv = tE.n

            def applyQ(e):
                last = None
                for c in range(DC):
                    last = e.tensor_tensor(Q1[:, c, :], z2b[:, c, :],
                                           B4[1][:, :], ALU.mult)
                return last
            on("vector", applyQ, waits=[(tE, bcv)], inc=vE)
            return (z2ready, vE.n)

        XBv = z2b
        XSqv = zb
        pre_state = attna_pre(0) if (PHASES >= 2 and A_QT > 0) else None
        for qt in range((NQT if PHASES >= 2 else 0) if A_QT >= 4 else A_QT):
            sl_ = slice(qt * QT, (qt + 1) * QT)
            z2ready, qready = pre_state
            if A_STAGE < 3:
                if qt + 1 < NQT:
                    pre_state = attna_pre(qt + 1)
                continue

            F0h = [F0[:, 0:2048], F0[:, 2048:4096]]
            F2h = [F2[:, 0:2048], F2[:, 2048:4096]]
            sgh = [sgf[:, 0:2048], sgf[:, 2048:4096]]
            chain_vu = [0, 0]
            chain_vF0 = [z2ready, z2ready]
            Bv = [0, 0, 0, 0]
            for pair in range(4):
                pr_sg = []
                for gh in range(2):
                    g = 2 * pair + gh
                    kb = knb[g % 2]
                    wk = kn_ld_hist[-2][1] if len(kn_ld_hist) >= 2 else 0
                    on("sync", lambda e, g=g, kb=kb: e.dma_start(
                        out=kb[:, :, :],
                        in_=knbfv[:, :, g * QT:(g + 1) * QT]),
                        waits=[(tE, wk)] + P0DONE, inc=knLs[g % 2], inc_by=16)
                    kn_ld_hist.append((knLs[g % 2].n, tE.n))
                    for it4 in range(4):
                        it = g * 4 + it4
                        slot = it % 4
                        bank = sb4[slot]
                        wv_, ws_ = evac_hist[-4] if len(evac_hist) >= 4 else (0, 0)

                        def smm(e, bank=bank, it4=it4, kb=kb):
                            last = None
                            for c in range(DC):
                                last = e.matmul(
                                    bank, kb[:, c, it4 * 128:(it4 + 1) * 128],
                                    Q1[:, c, :],
                                    start=(c == 0), stop=(c == DC - 1))
                            return last
                        on("tensor", smm,
                           waits=[(knLs[g % 2], kn_ld_hist[-1][0]),
                                  (vE, qready), (sE, ws_)], inc=tE)
                        smv = tE.n
                        on("scalar", lambda e, bank=bank, gh=gh, it4=it4:
                            e.activation(sgb[:, gh * 4 + it4, :], bank, AF.Sign),
                            waits=[(tE, smv), (vE, chain_vu[gh])], inc=sE)
                        on("scalar", lambda e, bank=bank, gh=gh, it4=it4:
                            e.activation(F0v[:, gh * 4 + it4, :], bank, AF.Abs),
                            waits=[(vE, chain_vF0[gh])], inc=sE)
                        evac_hist.append((vE.n, sE.n))
                    kn_ld_hist[-1] = (kn_ld_hist[-1][0], tE.n)
                    pr_sg.append(sE.n)
                st = {}
                for gh in range(2):
                    on("vector", lambda e, gh=gh: e.tensor_scalar(
                        F0h[gh], F0h[gh], CLIP, None, ALU.min),
                        waits=[(sE, pr_sg[gh])], inc=vE)
                for gh in range(2):
                    on("vector", lambda e, gh=gh: e.scalar_tensor_tensor(
                        F2h[gh], F0h[gh], -1.0, F0h[gh], ALU.mult, ALU.mult),
                        inc=vE)
                    st[("na", gh)] = vE.n
                for gh in range(2):
                    on("scalar", lambda e, gh=gh: e.activation(
                        F2h[gh], F2h[gh], AF.Ln, bias=1.0),
                        waits=[(vE, st[("na", gh)])], inc=sE)
                for gh in range(2):
                    on("scalar", lambda e, gh=gh: e.activation(
                        F2h[gh], F2h[gh], AF.Exp, scale=-0.5), inc=sE)
                    st[("rc", gh)] = sE.n
                for gh in range(2):
                    on("vector", lambda e, gh=gh: e.tensor_scalar(
                        F0h[gh], F0h[gh], -1.0, 1.0, ALU.mult, ALU.add),
                        waits=[(sE, st[("rc", gh)])], inc=vE)
                for gh in range(2):
                    on("vector", lambda e, gh=gh: e.tensor_tensor(
                        F0h[gh], F0h[gh], F2h[gh], ALU.mult), inc=vE)
                    st[("w", gh)] = vE.n
                for gh in range(2):
                    on("scalar", lambda e, gh=gh: e.activation(
                        F2h[gh], F0h[gh], AF.Arctan),
                        waits=[(vE, st[("w", gh)])], inc=sE)
                    st[("at", gh)] = sE.n
                for gh in range(2):
                    on("vector", lambda e, gh=gh: e.tensor_scalar(
                        F0h[gh], F2h[gh], -2.0 / PI, 0.5, ALU.mult, ALU.add),
                        waits=[(sE, st[("at", gh)])], inc=vE)
                for gh in range(2):
                    on("vector", lambda e, gh=gh: e.tensor_tensor(
                        F2h[gh], F0h[gh], sgh[gh], ALU.mult), inc=vE)
                    st[("u", gh)] = vE.n
                    chain_vu[gh] = vE.n
                for gh in range(2):
                    on("scalar", lambda e, gh=gh: e.activation(
                        F0h[gh], F2h[gh], AF.Square, bias=c_half[:, :]),
                        waits=[(vE, st[("u", gh)])], inc=sE)
                for gh in range(2):
                    on("scalar", lambda e, gh=gh: e.activation(
                        F0h[gh], F0h[gh], AF.Square), inc=sE)
                for gh in range(2):
                    on("scalar", lambda e, gh=gh: e.activation(
                        F0h[gh], F0h[gh], AF.Square), inc=sE)
                    st[("t8", gh)] = sE.n
                for gh in range(2):
                    g = 2 * pair + gh
                    on("vector", lambda e, gh=gh, g=g: e.scalar_tensor_tensor(
                        Bbuf[:, 4 * g:4 * g + 4, :].rearrange("p a n -> p (a n)"),
                        F2h[gh], 0.5, F0h[gh], ALU.add, ALU.mult),
                        waits=[(sE, st[("t8", gh)])], inc=vE)
                    chain_vF0[gh] = vE.n
                Bv[pair] = vE.n
            b_ready = vE.n
            s_done = tE.n
            if qt == 0 and TAPS:
                tap_sbuf(d_tapB[:, :], Bbuf.rearrange("p a n -> p (a n)"))

            if qt + 1 < NQT and A_QT >= 4 and PHASES >= 2:
                pre_state = attna_pre(qt + 1)
            if A_STAGE < 4:
                continue

            half_free = (vE.n, sE.n)
            for half in range(2):
                for it in range(NKT):
                    qb = qwb[it % 2]
                    wq = qw_hist[-2] if len(qw_hist) >= 2 else 0
                    qwf = [(tE, wq)] if (it >= 2 or half > 0) else \
                        [(tE, wq), (vE, qready)]
                    on("sync", lambda e, it=it, half=half, qb=qb: e.dma_start(
                        out=qb[:, :],
                        in_=s_qwbf[it * 128:(it + 1) * 128,
                                   half * 512:(half + 1) * 512]),
                        waits=qwf + P0DONE, inc=qwLs[it % 2], inc_by=16)
                    ld = (qwLs[it % 2], qwLs[it % 2].n)

                    def xmm(e, it=it, qb=qb):
                        last = None
                        for c in range(4):
                            last = e.matmul(sb4[c], qb[:, c * 128:(c + 1) * 128],
                                            Bbuf[:, it, :],
                                            start=(it == 0), stop=(it == NKT - 1))
                        return last
                    on("tensor", xmm,
                       waits=[ld, (vE, max(Bv[it // 8], half_free[0])),
                              (sE, half_free[1])], inc=tE)
                    qw_hist.append(tE.n)
                xdone = tE.n
                on("scalar", lambda e, half=half: e.activation(
                    XSqv[:, 4 * half:4 * half + 4, :].rearrange("p a n -> p (a n)"),
                    PS4[:, :], AF.Square), waits=[(tE, xdone)], inc=sE)
                on("vector", lambda e, half=half: e.tensor_copy(
                    XBv[:, 4 * half:4 * half + 4, :].rearrange("p a n -> p (a n)"),
                    PS4[:, :]), waits=[(tE, xdone), (sE, sE.n)], inc=vE)
                half_free = (vE.n, sE.n)

            if qt == 0 and TAPS:
                tap_sbuf(d_tapX[:, :], XBv.rearrange("p a n -> p (a n)"))
            if A_STAGE < 5:
                continue

            def statmmX(e):
                last = None
                for c in range(DC):
                    last = e.matmul(B4[0][0:1, :], ones_col[:, :], XSqv[:, c, :],
                                    start=(c == 0), stop=(c == DC - 1))
                return last
            on("tensor", statmmX,
               waits=[(sE, half_free[1]), (vE, half_free[0])], inc=tE)
            on("vector", lambda e: e.tensor_scalar(
                rx0[:, :], B4[0][0:1, :], EPS24, None, ALU.max),
                waits=[(tE, tE.n)], inc=vE)
            on("scalar", lambda e: e.activation(rx1[:, :], rx0[:, :], AF.Ln),
               waits=[(vE, vE.n)], inc=sE)
            on("scalar", lambda e: e.activation(rx0[:, :], rx1[:, :], AF.Exp,
                                                scale=-0.5), inc=sE)
            sr = sE.n
            on("tensor", lambda e: e.matmul(B4[1][:, :], ones_rowD[0:1, :],
                                            rx0[0:1, :], start=True, stop=True),
               waits=[(sE, sr)], inc=tE)
            bcv = tE.n

            def applyresA(e):
                last = None
                for c in range(DC):
                    last = e.tensor_tensor(TMPv[:, c, :], XBv[:, c, :],
                                           B4[1][:, :], ALU.mult)
                return last

            def applyresB(e, sl_=sl_):
                last = None
                for c in range(DC):
                    last = e.scalar_tensor_tensor(
                        xT[:, c, sl_], xT[:, c, sl_], p_aab[:, c:c + 1],
                        TMPv[:, c, :], ALU.add, ALU.add)
                return last
            on("vector", applyresA, waits=[(tE, bcv)], inc=vE)
            on("vector", applyresB, inc=vE)

        tapsem = Sem(E(nc.semaphore("tapsem"))) if TAPS else None

        def emit_tap(i):
            if not (TAPS and (TAPS & (1 << i))):
                return
            light = bool(TAPS & 8)
            w = [(vE, vE.n)] if light else [(vE, vE.n), (tE, tE.n), (sE, sE.n)]
            for c in range(DC):
                on("gpsimd", lambda e, c=c, i=i: e.dma_start(
                    out=d_taps[i][c * 128:(c + 1) * 128, :], in_=xT[:, c, :]),
                    waits=w, inc=tapsem, inc_by=16)
            if not light:
                on("vector", lambda e: e.engine_nop(),
                   waits=[(tapsem, tapsem.n)], inc=vE)

        emit_tap(0)

        wq1v = s_wq1.rearrange("(c p) f -> p c f", p=128)
        wq2v = s_wq2.rearrange("(c p) f -> p c f", p=128)

        def emit_ffn():
            ev_free = {0: 0, 1: 0}
            init_tE, init_vE = tE.n, vE.n
            relu_hist = []
            w1_hist = []
            w2_hist = []
            ybufs = [zb, z2b]
            pend = emit_ln(0, zout=ybufs[0])
            for qt in range(NQT):
                sl_ = slice(qt * QT, (qt + 1) * QT)
                zready = pend
                yb = ybufs[qt % 2]
                for ft2 in range(NKT // 2):
                    buf = w1s[ft2 % 2]
                    wk = w1_hist[-2] if len(w1_hist) >= 2 else init_tE
                    on("sync", lambda e, ft2=ft2, buf=buf: e.dma_start(
                        out=buf[:, :, :],
                        in_=wq1v[:, :, ft2 * 256:(ft2 + 1) * 256]),
                        waits=[(tE, wk), (vE, init_vE)] + P0DONE,
                        inc=w1Ls[ft2 % 2], inc_by=16)
                    wld = (w1Ls[ft2 % 2], w1Ls[ft2 % 2].n)
                    for k in range(2):
                        ft = ft2 * 2 + k
                        bank = sb4[ft % 4]
                        rw = relu_hist[-4] if len(relu_hist) >= 4 else 0

                        def m1(e, bank=bank, buf=buf, k=k, yb=yb):
                            last = None
                            for c in range(DC):
                                last = e.matmul(
                                    bank, buf[:, c, k * 128:(k + 1) * 128],
                                    yb[:, c, :], start=(c == 0),
                                    stop=(c == DC - 1))
                            return last
                        on("tensor", m1,
                           waits=[wld, (vE, zready), (sE, rw)], inc=tE)
                        m1v = tE.n
                        on("scalar", lambda e, bank=bank, ft=ft: e.activation(
                            hbuf[:, ft, :], bank, AF.Relu,
                            bias=p_b1[:, ft:ft + 1]), waits=[(tE, m1v)], inc=sE)
                        relu_hist.append(sE.n)
                    w1_hist.append(tE.n)
                hready = sE.n
                if qt + 1 < NQT:
                    pend = emit_ln(qt + 1, zout=ybufs[(qt + 1) % 2])
                for dd in range(DC):
                    buf = w2s[dd % 2]
                    wk = w2_hist[-2] if len(w2_hist) >= 2 else init_tE
                    on("sync", lambda e, dd=dd, buf=buf: e.dma_start(
                        out=buf[:, :, :],
                        in_=wq2v[:, :, dd * 128:(dd + 1) * 128]),
                        waits=[(tE, wk), (vE, init_vE)] + P0DONE,
                        inc=w2Ls[dd % 2], inc_by=16)
                    wld = (w2Ls[dd % 2], w2Ls[dd % 2].n)
                    bank = B4[dd % 2][:, :]

                    def m2(e, bank=bank, buf=buf):
                        last = None
                        for fc in range(NKT):
                            last = e.matmul(bank, buf[:, fc, :], hbuf[:, fc, :],
                                            start=(fc == 0), stop=(fc == NKT - 1))
                        return last
                    on("tensor", m2,
                       waits=[wld, (sE, hready), (vE, ev_free[dd % 2])],
                       inc=tE)
                    m2v = tE.n
                    on("vector", lambda e, bank=bank, dd=dd, sl_=sl_:
                        e.scalar_tensor_tensor(
                            xT[:, dd, sl_], xT[:, dd, sl_], p_b2[:, dd:dd + 1],
                            bank, ALU.add, ALU.add),
                        waits=[(tE, m2v)], inc=vE)
                    ev_free[dd % 2] = vE.n
                    w2_hist.append(tE.n)

        if PHASES >= 3:
            emit_ffn()
        emit_tap(1)

        wqkvbv = s_wqkvbf.rearrange("(c p) f -> p c f", p=128)
        on("sync", lambda e: e.dma_start(
            out=wv[:, :, :], in_=wqkvbv[:, :, 2048:3072]),
            waits=[(tE, tE.n)] + P0DONE, inc=wvLs, inc_by=16)
        WV_ld = (wvLs, wvLs.n)

        qk_hist = []
        kv_dma = 0
        ATTNB_vE0 = vE.n
        for qt in range(NQT if PHASES >= 4 else 0):
            sl_ = slice(qt * QT, (qt + 1) * QT)
            zready = emit_ln(qt)
            ev_qk = 0
            for cc in range(16):
                buf = wqs[cc % 2]
                wk = qk_hist[-2] if len(qk_hist) >= 2 else 0
                on("sync", lambda e, cc=cc, buf=buf: e.dma_start(
                    out=buf[:, :, :], in_=wqkvbv[:, :, cc * 128:(cc + 1) * 128]),
                    waits=[(tE, wk), (vE, ATTNB_vE0)] + P0DONE,
                    inc=wqLs[cc % 2], inc_by=16)
                wld = (wqLs[cc % 2], wqLs[cc % 2].n)
                bank = B4[cc % 2][:, :]

                def qkm(e, bank=bank, buf=buf):
                    last = None
                    for c in range(DC):
                        last = e.matmul(bank, buf[:, c, :], zb[:, c, :],
                                        start=(c == 0), stop=(c == DC - 1))
                    return last
                on("tensor", qkm,
                   waits=[wld, (vE, max(zready, ev_qk))], inc=tE)
                qv = tE.n
                if cc < 8:
                    on("vector", lambda e, bank=bank, cc=cc, sl_=sl_:
                        e.tensor_scalar(
                            QTb[:, cc, sl_], bank, p_bqkv[:, cc:cc + 1], 0.125,
                            ALU.add, ALU.mult), waits=[(tE, qv)], inc=vE)
                else:
                    on("vector", lambda e, bank=bank, cc=cc: e.tensor_scalar(
                        kstg[:, :], bank, p_bqkv[:, cc:cc + 1], None, ALU.add),
                        waits=[(tE, qv), (ksSs, ksSs.n)], inc=vE)
                    on("scalar", lambda e, cc=cc, sl_=sl_: e.dma_start(
                        out=cc_inK.ap()[(cc - 8) * 128:(cc - 7) * 128, sl_],
                        in_=kstg[:, :]), waits=[(vE, vE.n)], inc=ksSs, inc_by=16)
                ev_qk = vE.n
                qk_hist.append(tE.n)
            vp_free = {0: 0, 1: 0}
            for tc in range(4):
                h0 = (tc % 2) * 1024
                vb0 = PS4[:, h0:h0 + 512]
                vb1 = PS4[:, h0 + 512:h0 + 1024]

                def vpre(e, vb0=vb0, vb1=vb1):
                    e.matmul(vb0, ones_rowD[0:1, :], p_bv[0:1, 0:512],
                             start=True, stop=False)
                    return e.matmul(vb1, ones_rowD[0:1, :],
                                    p_bv[0:1, 512:1024], start=True, stop=False)
                on("tensor", vpre,
                   waits=[WV_ld, (vE, vp_free[tc % 2])], inc=tE)

                def vmm(e, tc=tc, vb0=vb0, vb1=vb1):
                    last = None
                    for c in range(DC):
                        stat = zb[:, c, tc * 128:(tc + 1) * 128]
                        last = e.matmul(vb0, stat, wv[:, c, 0:512],
                                        start=False, stop=(c == DC - 1))
                        last = e.matmul(vb1, stat,
                                        wv[:, c, 512:1024],
                                        start=False, stop=(c == DC - 1))
                    return last
                on("tensor", vmm, inc=tE)
                vmv = tE.n
                vbuf = vplain[tc % 2]
                on("vector", lambda e, vbuf=vbuf, h0=h0: e.tensor_copy(
                    vbuf[:, :], PS4[:, h0:h0 + 1024]),
                    waits=[(tE, vmv), (vpSs[tc % 2], vpSs[tc % 2].n)], inc=vE)
                on("scalar", lambda e, vbuf=vbuf, qt=qt, tc=tc: e.dma_start(
                    out=cc_inV.ap()[qt * 512 + tc * 128:
                                    qt * 512 + (tc + 1) * 128, :],
                    in_=vbuf[:, :]), waits=[(vE, vE.n)],
                    inc=vpSs[tc % 2], inc_by=16)
                vp_free[tc % 2] = vE.n

        if PHASES >= 5:
            on("gpsimd", lambda e: e.collective_compute(
                "AllGather", ALU.bypass,
                replica_groups=[[0, 1], [2, 3], [4, 5], [6, 7]],
                ins=[cc_inK.ap().opt()], outs=[cc_outK.ap().opt()]),
                waits=[(ksSs, ksSs.n)], inc=ccK)
            on("gpsimd", lambda e: e.collective_compute(
                "AllGather", ALU.bypass,
                replica_groups=[[0, 1], [2, 3], [4, 5], [6, 7]],
                ins=[cc_inV.ap().opt()], outs=[cc_outV.ap().opt()]),
                waits=[(vpSs[0], vpSs[0].n), (vpSs[1], vpSs[1].n)], inc=ccV)
        GV = gE.n

        ccoutVv = cc_outV.ap().rearrange("(k p) d -> p k d", p=128)
        exp_hist = []
        av_hist = []
        pair_ev = 0
        pair_te = []
        ps4_guard = vE.n
        woutbv = s_woutbf.rearrange("(c p) f -> p c f", p=128)
        op_hist = []
        op_ev = {}
        QKV_TE = tE.n
        for qt in range(NQT if PHASES >= 5 else 0):
            for p in range(8):
                bslot = p % 2
                wk = pair_te[-2] if len(pair_te) >= 2 else QKV_TE
                kb = KTs[bslot]
                on("sync", lambda e, p=p, kb=kb: e.dma_start(
                    out=kb[:, 0:2048],
                    in_=cc_outK.ap()[p * 128:(p + 1) * 128, :]),
                    waits=[(ccK, 1), (tE, wk), (vpSs[0], vpSs[0].n),
                           (vpSs[1], vpSs[1].n), (ksSs, ksSs.n)],
                    inc=ktLs[bslot], inc_by=16)
                on("sync", lambda e, p=p, kb=kb: e.dma_start(
                    out=kb[:, 2048:4096],
                    in_=cc_outK.ap()[1024 + p * 128:1024 + (p + 1) * 128, :]),
                    inc=ktLs[bslot], inc_by=16)
                ktv = (ktLs[bslot], ktLs[bslot].n)
                vx = Vxs[bslot]
                h0, h1 = 2 * p, 2 * p + 1
                on("scalar", lambda e, h0=h0, vx=vx: e.dma_start(
                    out=vx[:, :, :], in_=ccoutVv[:, :, h0 * 64:(h0 + 2) * 64]),
                    waits=[(ccV, 1), (gE, GV), (tE, wk)],
                    inc=vxLs[bslot], inc_by=16)
                vxv = (vxLs[bslot], vxLs[bslot].n)

                pend_av = None
                for kc in range(NKT):
                    sp = PS4[:, (kc % 2) * 1024:(kc % 2 + 1) * 1024]
                    pb = Pbuf[kc % 2]
                    we = exp_hist[-2] if len(exp_hist) >= 2 else 0
                    wa = av_hist[-1] if len(av_hist) >= 1 else 0

                    def smm(e, sp=sp, kc=kc, kb=kb, qt=qt, p=p):
                        e.matmul(sp[:, 0:512],
                                 kb[0:64, kc * 128:(kc + 1) * 128],
                                 QTb[0:64, p, qt * QT:(qt + 1) * QT],
                                 start=True, stop=True, tile_position=(0, 0))
                        return e.matmul(
                            sp[:, 512:1024],
                            kb[64:128, kc * 128:(kc + 1) * 128],
                            QTb[64:128, p, qt * QT:(qt + 1) * QT],
                            start=True, stop=True, tile_position=(64, 0))
                    on("tensor", smm,
                       waits=[ktv, (sE, we), (vE, ps4_guard)], inc=tE)
                    smv = tE.n
                    on("scalar", lambda e, sp=sp, pb=pb: e.activation(
                        pb.rearrange("p h n -> p (h n)"), sp[:, :], AF.Exp),
                        waits=[(tE, smv), (tE, wa)], inc=sE)
                    exp_hist.append(sE.n)
                    sev = sE.n
                    if pend_av is not None:
                        on("tensor", pend_av[0], waits=pend_av[1], inc=tE)
                        av_hist.append(tE.n)

                    def avm(e, pb=pb, vx=vx, kc=kc):
                        e.matmul(B4[0][0:64, :], vx[:, kc, 0:64], pb[:, 0, :],
                                 start=(kc == 0), stop=(kc == NKT - 1),
                                 tile_position=(0, 0))
                        e.matmul(B4[1][64:128, :], vx[:, kc, 64:128],
                                 pb[:, 1, :],
                                 start=(kc == 0), stop=(kc == NKT - 1),
                                 tile_position=(0, 64))
                        e.matmul(B4[2][0:1, :], ones_col[:, :], pb[:, 0, :],
                                 start=(kc == 0), stop=(kc == NKT - 1),
                                 tile_position=(0, 0))
                        return e.matmul(
                            B4[3][32:33, :], ones_col[:, :], pb[:, 1, :],
                            start=(kc == 0), stop=(kc == NKT - 1),
                            tile_position=(0, 32))
                    pend_av = (avm, [vxv, (sE, sev), (vE, pair_ev)])
                on("tensor", pend_av[0], waits=pend_av[1], inc=tE)
                av_hist.append(tE.n)
                av_done = tE.n
                on("scalar", lambda e: e.activation(
                    r_t1[0:1, :], B4[2][0:1, :], AF.Ln),
                    waits=[(tE, av_done)], inc=sE)
                on("scalar", lambda e: e.activation(
                    r_rstd[0:1, :], r_t1[0:1, :], AF.Exp, scale=-1.0), inc=sE)
                on("scalar", lambda e: e.activation(
                    r_t1[0:1, :], B4[3][32:33, :], AF.Ln), inc=sE)
                on("scalar", lambda e: e.activation(
                    r_nmr[0:1, :], r_t1[0:1, :], AF.Exp, scale=-1.0), inc=sE)
                rv = sE.n

                def bcr(e):
                    e.matmul(B4[3][0:64, :], ones_rowD[0:1, 0:64],
                             r_rstd[0:1, :], start=True, stop=True,
                             tile_position=(0, 0))
                    return e.matmul(B4[2][64:128, :], ones_rowD[0:1, 0:64],
                                    r_nmr[0:1, :], start=True, stop=True,
                                    tile_position=(0, 64))
                on("tensor", bcr, waits=[(sE, rv), (vE, pair_ev)], inc=tE)
                bv_ = tE.n
                on("vector", lambda e: e.tensor_copy(
                    Obstg[0:64, :], B4[0][0:64, :]), waits=[(tE, bv_)], inc=vE)
                on("vector", lambda e: e.tensor_copy(
                    Obstg[64:128, :], B4[1][64:128, :]), inc=vE)
                on("vector", lambda e, p=p: e.tensor_tensor(
                    attno[0:64, p, :], Obstg[0:64, :], B4[3][0:64, :],
                    ALU.mult), inc=vE)
                on("vector", lambda e, p=p: e.tensor_tensor(
                    attno[64:128, p, :], Obstg[64:128, :], B4[2][64:128, :],
                    ALU.mult), inc=vE)
                pair_ev = vE.n
                pair_te.append(tE.n)
            sl_ = slice(qt * QT, (qt + 1) * QT)
            for dd in range(DC):
                buf = wobs[dd % 2]
                wk = op_hist[-2] if len(op_hist) >= 2 else QKV_TE
                on("sync", lambda e, dd=dd, buf=buf: e.dma_start(
                    out=buf[:, :, :], in_=woutbv[:, :, dd * 128:(dd + 1) * 128]),
                    waits=[(tE, wk)] + P0DONE, inc=opLs[dd % 2], inc_by=16)
                wld = (opLs[dd % 2], opLs[dd % 2].n)
                bank = PS4[:, (dd % 4) * 512:(dd % 4 + 1) * 512]

                def opm2(e, bank=bank, buf=buf, dd=dd):
                    last = e.matmul(bank, p_bout[0:1, dd * 128:(dd + 1) * 128],
                                    ones_rowN[0:1, :], start=True, stop=False)
                    for c in range(DC):
                        last = e.matmul(bank, buf[:, c, :], attno[:, c, :],
                                        start=False, stop=(c == DC - 1))
                    return last
                on("tensor", opm2,
                   waits=[wld, (vE, pair_ev), (vE, op_ev.get(dd % 4, 0))],
                   inc=tE)
                opv = tE.n
                on("vector", lambda e, bank=bank, dd=dd, sl_=sl_: e.tensor_tensor(
                    xT[:, dd, sl_], xT[:, dd, sl_], bank, ALU.add),
                    waits=[(tE, opv)], inc=vE)
                op_ev[dd % 4] = vE.n
                op_hist.append(tE.n)
            ps4_guard = vE.n

        emit_tap(2)
        if PHASES >= 6:
            emit_ffn()

        FIN = vE.n
        for c in range(DC):
            on("gpsimd", lambda e, c=c: e.dma_start(
                out=d_out[c * 128:(c + 1) * 128, :], in_=xT[:, c, :]),
                waits=[(vE, FIN)], inc=outQ, inc_by=16)
        on("gpsimd", lambda e: e.engine_nop(), waits=[(outQ, outQ.n)], inc=gE)

        with nc.Block() as block:
            @block.sync
            def _(e):
                for f in P.steps["sync"]:
                    f(e)

            @block.tensor
            def _(e):
                for f in P.steps["tensor"]:
                    f(e)

            @block.vector
            def _(e):
                for f in P.steps["vector"]:
                    f(e)

            @block.scalar
            def _(e):
                for f in P.steps["scalar"]:
                    f(e)

            @block.gpsimd
            def _(e):
                for f in P.steps["gpsimd"]:
                    f(e)

    return nc


_cache = {}


def kernel(**inputs):
    x = np.asarray(inputs["L0_attn_a_hidden_states"])
    kw = np.asarray(inputs["L0_attn_a_k_weight"])
    qw = np.asarray(inputs["L0_attn_a_q_weight"])
    lnw = np.asarray(inputs["L0_attn_a_ln_weight"])
    lnb = np.asarray(inputs["L0_attn_a_ln_bias"])
    aab = np.asarray(inputs["L0_attn_a_bias"])
    wqkv = np.asarray(inputs["L0_attn_b_w_qkv"])
    bqkv = np.asarray(inputs["L0_attn_b_b_qkv"])
    wout = np.asarray(inputs["L0_attn_b_w_out"])
    bout = np.asarray(inputs["L0_attn_b_b_out"])
    W1 = np.asarray(inputs["L0_ffn_W1"])
    b1 = np.asarray(inputs["L0_ffn_b1"])
    W2 = np.asarray(inputs["L0_ffn_W2"])
    b2 = np.asarray(inputs["L0_ffn_b2"])

    if "nc" not in _cache:
        _cache["nc"] = build_program()
    nc = _cache["nc"]

    com = {
        "knTw": np.ascontiguousarray(kw.T),
        "qw": np.ascontiguousarray(qw),
        "wqkv": np.ascontiguousarray(wqkv),
        "wout": np.ascontiguousarray(wout),
        "w1T": np.ascontiguousarray(W1.T),
        "w2T": np.ascontiguousarray(W2.T),
        "lnw_p": np.ascontiguousarray(lnw.reshape(DC, 128).T),
        "lnb_p": np.ascontiguousarray(lnb.reshape(DC, 128).T),
        "aab_p": np.ascontiguousarray(aab.reshape(DC, 128).T),
        "bqkv_p": np.ascontiguousarray(bqkv[:2048].reshape(16, 128).T),
        "bv_row": np.ascontiguousarray(bqkv[2048:].reshape(1, D)),
        "b1_p": np.ascontiguousarray(b1.reshape(NKT, 128).T),
        "b2_p": np.ascontiguousarray(b2.reshape(DC, 128).T),
        "bout_row": np.ascontiguousarray(bout.reshape(1, D)),
    }
    in_maps = []
    for core in range(8):
        b, h = core // 2, core % 2
        xs = np.ascontiguousarray(x[b, h * TPC:(h + 1) * TPC, :].T)
        m = dict(com)
        m["xT"] = xs
        in_maps.append(m)

    res = run_bass_kernel_spmd(nc, in_maps, core_ids=list(range(8)))
    out = np.empty((4, 4096, D), np.float32)
    for core in range(8):
        b, h = core // 2, core % 2
        out[b, h * TPC:(h + 1) * TPC, :] = res.results[core]["outT"].T
    return out


# version 16
# speedup vs baseline: 1.0084x; 1.0084x over previous
import numpy as np
import ml_dtypes
from contextlib import ExitStack

import concourse.bass as bass
import concourse.mybir as mybir
from concourse.bass_utils import run_bass_kernel_spmd

F32 = mybir.dt.float32
BF16 = mybir.dt.bfloat16
AF = mybir.ActivationFunctionType
ALU = mybir.AluOpType

D = 1024
DC = 8
TPC = 2048
NQT = 4
QT = 512
NKT = 32
DFF = 4096
MAGIC = float(np.float32(3 * 2 ** 22))
CLIP = float(np.float32(1.0 - 1e-6))
INV1024 = float(np.float32(1.0 / 1024.0))
EPS5 = 1e-5
EPS12 = 1e-12
EPS24 = 1e-24
PI = float(np.pi)

ENGINES = ("sync", "tensor", "vector", "scalar", "gpsimd")

SAFE_EDGES = False
TAPS = False
PHASES = 6
A_QT = 4
A_STAGE = 5


class Sem:
    def __init__(self, h):
        self.h = h
        self.n = 0


class Prog:
    def __init__(self):
        self.steps = {e: [] for e in ENGINES}
        self.cur = {e: {} for e in ENGINES}

    def on(self, eng, fn, waits=(), inc=None, inc_by=1):
        ss = getattr(self, "selfsems", {}).get(eng) if SAFE_EDGES else None
        if ss is not None:
            waits = list(waits) + [(ss, ss.n)]
            if inc is None:
                inc = ss
        wl = []
        cur = self.cur[eng]
        for s, v in waits:
            if v <= 0:
                continue
            k = id(s)
            if cur.get(k, -1) < v:
                cur[k] = v
                wl.append((s.h, v))
        if inc is not None:
            inc.n += inc_by

        def run(e, wl=wl, fn=fn, inc=inc, inc_by=inc_by):
            for h, v in wl:
                e.wait_ge(h, v)
            ins = fn(e)
            if inc is not None:
                ins.then_inc(inc.h, inc_by)
        self.steps[eng].append(run)


def build_program():
    nc = bass.Bass("TRN2", target_bir_lowering=False)

    def din(name, shape):
        return nc.dram_tensor(name, shape, F32, kind="ExternalInput").ap()

    d_xT = din("xT", [D, TPC])
    d_knT = din("knTw", [D, DFF])
    d_qw = din("qw", [DFF, D])
    d_wqkv = din("wqkv", [D, 3 * D])
    d_wout = din("wout", [D, D])
    d_w1T = din("w1T", [D, DFF])
    d_w2T = din("w2T", [DFF, D])
    d_lnw = din("lnw_p", [128, DC])
    d_lnb = din("lnb_p", [128, DC])
    d_aab = din("aab_p", [128, DC])
    d_bqkv = din("bqkv_p", [128, 16])
    d_bv = din("bv_row", [1, D])
    d_b1 = din("b1_p", [128, NKT])
    d_b2 = din("b2_p", [128, DC])
    d_bout = din("bout_row", [1, D])
    d_out = nc.dram_tensor("outT", [D, TPC], F32, kind="ExternalOutput").ap()

    _wk = "ExternalOutput" if TAPS else "Internal"
    s_qwbf = nc.dram_tensor("qw_bf", [DFF, D], BF16, kind=_wk).ap()
    s_wqkvbf = nc.dram_tensor("wqkv_bf", [D, 3 * D], BF16, kind=_wk).ap()
    s_woutbf = nc.dram_tensor("wout_bf", [D, D], BF16, kind=_wk).ap()
    s_knbf = nc.dram_tensor("knT_bf", [D, DFF], BF16, kind=_wk).ap()
    s_wq1 = nc.dram_tensor("wq1T_bf", [D, DFF], BF16, kind=_wk).ap()
    s_wq2 = nc.dram_tensor("wq2T_bf", [DFF, D], BF16, kind=_wk).ap()
    cc_inK = nc.dram_tensor("cc_inK", [D, TPC], BF16)
    cc_outK = nc.dram_tensor("cc_outK", [2 * D, TPC], BF16)
    cc_inV = nc.dram_tensor("cc_inV", [TPC, D], BF16)
    cc_outV = nc.dram_tensor("cc_outV", [2 * TPC, D], BF16)
    d_taps = [nc.dram_tensor(f"tap{i}", [D, TPC], F32, kind="ExternalOutput").ap()
              for i in range(3)] if TAPS else None
    d_tapQ = nc.dram_tensor("tapQ", [128, 4096], BF16,
                            kind="ExternalOutput").ap() if TAPS else None
    d_tapB = nc.dram_tensor("tapB", [128, 16384], BF16,
                            kind="ExternalOutput").ap() if TAPS else None
    d_tapX = nc.dram_tensor("tapX", [128, 4096], BF16,
                            kind="ExternalOutput").ap() if TAPS else None
    d_tapG = nc.dram_tensor("tapG", [128, 8], F32,
                            kind="ExternalOutput").ap() if TAPS else None

    P = Prog()
    on = P.on

    with ExitStack() as ctx:
        E = ctx.enter_context

        xT = E(nc.sbuf_tensor("s_xT", [128, DC, TPC], F32))
        AR32 = E(nc.sbuf_tensor("AR32", [128, 8 * 1024], F32))
        ARB = E(nc.sbuf_tensor("ARB", [128, 45 * 1024], BF16))
        ones_col = E(nc.sbuf_tensor("ones_col", [128, 1], BF16))
        ones_colf = E(nc.sbuf_tensor("ones_colf", [128, 1], F32))
        ones_rowD = E(nc.sbuf_tensor("ones_rowD", [1, 128], F32))
        ones_rowN = E(nc.sbuf_tensor("ones_rowN", [1, QT], F32))
        p_lnw = E(nc.sbuf_tensor("p_lnw", [128, DC], F32))
        p_lnb = E(nc.sbuf_tensor("p_lnb", [128, DC], F32))
        p_aab = E(nc.sbuf_tensor("p_aab", [128, DC], F32))
        p_bqkv = E(nc.sbuf_tensor("p_bqkv", [128, 16], F32))
        p_bv = E(nc.sbuf_tensor("p_bv", [1, D], F32))
        p_b1 = E(nc.sbuf_tensor("p_b1", [128, NKT], F32))
        p_b2 = E(nc.sbuf_tensor("p_b2", [128, DC], F32))
        p_bout = E(nc.sbuf_tensor("p_bout", [1, D], F32))
        p_gsc = E(nc.sbuf_tensor("p_gsc", [128, 2], F32))
        r_mu = E(nc.sbuf_tensor("r_mu", [1, QT], F32))
        r_t1 = E(nc.sbuf_tensor("r_t1", [1, QT], F32))
        r_t2 = E(nc.sbuf_tensor("r_t2", [1, QT], F32))
        r_rstd = E(nc.sbuf_tensor("r_rstd", [1, QT], F32))
        r_nmr = E(nc.sbuf_tensor("r_nmr", [1, QT], F32))
        rx0 = r_t1
        rx1 = r_t2
        r_g = E(nc.sbuf_tensor("r_g", [1, 4], F32))
        c_acc = E(nc.sbuf_tensor("c_acc", [128, 32], F32))
        c_acc1 = E(nc.sbuf_tensor("c_acc1", [128, 1], F32))
        c_half = E(nc.sbuf_tensor("c_half", [128, 1], F32))
        c_eps5 = E(nc.sbuf_tensor("c_eps5", [1, 1], F32))
        c_eps12 = E(nc.sbuf_tensor("c_eps12", [1, 1], F32))

        PS4 = E(nc.psum_tensor("PS4", [128, 2048], F32))
        B4 = [E(nc.psum_tensor(f"B4_{i}", [128, 512], F32)) for i in range(4)]

        tE = Sem(E(nc.semaphore("tE")))
        vE = Sem(E(nc.semaphore("vE")))
        sE = Sem(E(nc.semaphore("sE")))
        gE = Sem(E(nc.semaphore("gE")))
        ccK = Sem(E(nc.semaphore("ccK")))
        ccV = Sem(E(nc.semaphore("ccV")))
        P.selfsems = {"tensor": tE, "vector": vE, "scalar": sE, "gpsimd": gE}

        def mksem(name):
            return Sem(E(nc.semaphore(name)))
        ldXs = mksem("ldXs")
        p0L = [mksem("p0L0"), mksem("p0L1"), mksem("p0L2")]
        p0S = [mksem("p0S0"), mksem("p0S1"), mksem("p0S2")]
        knLs = [mksem("knL0"), mksem("knL1")]
        qwLs = [mksem("qwL0"), mksem("qwL1")]
        w1Ls = [mksem("w1L0"), mksem("w1L1")]
        w2Ls = [mksem("w2L0"), mksem("w2L1")]
        wqLs = [mksem("wqL0"), mksem("wqL1")]
        wvLs = mksem("wvLs")
        ktLs = [mksem("ktL0"), mksem("ktL1")]
        opLs = [mksem("opL0"), mksem("opL1")]
        vxLs = [mksem("vxL0"), mksem("vxL1")]
        ksSs = mksem("ksSs")
        vpSs = [mksem("vpS0"), mksem("vpS1")]
        outQ = mksem("outQ")

        F0 = AR32[:, 0:4096]
        F2 = AR32[:, 4096:8192]
        F0v = F0.rearrange("p (g n) -> p g n", n=QT)
        F2v = F2.rearrange("p (g n) -> p g n", n=QT)
        TMPv = F2v

        def av(lo, hi, inner=None):
            a = ARB[:, lo:hi]
            if inner is not None:
                a = a.rearrange("p (a n) -> p a n", n=inner)
            return a
        K = 1024
        xbf = av(0, 4 * K, QT)
        xsq = av(4 * K, 8 * K, QT)
        Q1 = xsq
        zb = av(8 * K, 12 * K, QT)
        z2b = av(12 * K, 16 * K, QT)
        sgb = av(16 * K, 20 * K, QT)
        sgf = av(16 * K, 20 * K)
        Bbuf = av(20 * K, 36 * K, QT)
        knb = [av(36 * K + i * 4 * K, 36 * K + (i + 1) * 4 * K, QT)
               for i in range(2)]
        qwb = [av(44 * K + i * 512, 44 * K + (i + 1) * 512)
               for i in range(2)]
        w1s = [av(16 * K + i * 2 * K, 16 * K + (i + 1) * 2 * K, 256)
               for i in range(2)]
        hbuf = av(20 * K, 36 * K, QT)
        w2s = [av(36 * K + i * 4 * K, 36 * K + (i + 1) * 4 * K, 128)
               for i in range(2)]
        wqs = [av(16 * K + i * K, 16 * K + (i + 1) * K, 128)
               for i in range(2)]
        vplain = [av(12 * K + i * K, 12 * K + (i + 1) * K) for i in range(2)]
        kstg = av(14 * K, 14 * K + 512)
        QTb = av(18 * K, 34 * K, TPC)
        wv = av(36 * K, 44 * K, D)
        Pbuf = [av(i * K, (i + 1) * K, QT) for i in range(2)]
        Obstg = av(2 * K, 2 * K + 512)
        wobs = [av(2560 + i * K, 2560 + (i + 1) * K, 128)
                for i in range(2)]
        attno = av(4608, 4608 + 4 * K, QT)
        KTs = [av(10 * K + i * 4 * K, 10 * K + (i + 1) * 4 * K)
               for i in range(2)]
        Vxs = [av(34 * K + i * 4096, 34 * K + (i + 1) * 4096, 128)
               for i in range(2)]
        p0s = [av(32 * K, 36 * K), av(16 * K, 20 * K), av(20 * K, 24 * K)]

        for c in range(DC):
            on("sync", lambda e, c=c: e.dma_start(
                out=xT[:, c, :], in_=d_xT[c * 128:(c + 1) * 128, :]),
                inc=ldXs, inc_by=16)
        for dst, srcp in [(p_lnw, d_lnw), (p_lnb, d_lnb), (p_aab, d_aab),
                          (p_bqkv, d_bqkv), (p_bv, d_bv), (p_b1, d_b1),
                          (p_b2, d_b2), (p_bout, d_bout)]:
            on("sync", lambda e, dst=dst, srcp=srcp: e.dma_start(
                out=dst[:, :], in_=srcp[:, :]), inc=ldXs, inc_by=16)
        X_LOADED = ldXs.n

        on("gpsimd", lambda e: e.memset(ones_col[:, :], 1.0), inc=gE)
        on("gpsimd", lambda e: e.memset(ones_colf[:, :], 1.0), inc=gE)
        on("gpsimd", lambda e: e.memset(ones_rowD[:, :], 1.0), inc=gE)
        on("gpsimd", lambda e: e.memset(ones_rowN[:, :], 1.0), inc=gE)
        on("gpsimd", lambda e: e.memset(c_half[:, :], 0.5), inc=gE)
        on("gpsimd", lambda e: e.memset(c_eps5[:, :], EPS5), inc=gE)
        on("gpsimd", lambda e: e.memset(c_eps12[:, :], EPS12), inc=gE)
        GINIT = gE.n

        knTv = d_knT.rearrange("(c p) i -> p c i", p=128)
        knbfv = s_knbf.rearrange("(c p) i -> p c i", p=128)
        p0sv = [p0s[i].rearrange("p (a n) -> p a n", n=QT) for i in range(2)]
        bF = dict(v=0, s=0, st=0)
        aS = [dict(v=0, s=0, st=0, t=0), dict(v=0, s=0, st=0, t=0)]
        a_slot = dict(i=0)
        AR32b = ARB[:, 24 * K:32 * K].bitcast(F32)
        aFbuf = [F0, AR32b]
        a_stg = [0, 2]
        stepsA, stepsB = [], []

        def mk_cast(sap, dap, cols):
            def run():
                on("sync", lambda e: e.dma_start(out=F2[:, 0:cols], in_=sap),
                   waits=[(vE, bF["v"]), (sE, bF["s"]), (p0S[1], bF["st"])],
                   inc=p0L[1], inc_by=16)
                ld = (p0L[1], p0L[1].n)
                on("vector", lambda e: e.tensor_copy(
                    p0s[1][:, 0:cols], F2[:, 0:cols]), waits=[ld], inc=vE)
                bF["v"] = vE.n
                on("sync", lambda e: e.dma_start(out=dap, in_=p0s[1][:, 0:cols]),
                   waits=[(vE, vE.n)], inc=p0S[1], inc_by=16)
                bF["st"] = p0S[1].n
            return run

        for srcp, dstp, rows, cols in [(d_qw, s_qwbf, DFF, D),
                                       (d_wqkv, s_wqkvbf, D, 3 * D),
                                       (d_wout, s_woutbf, D, D)]:
            nc_ = rows // 128
            gsz = max(1, 4096 // cols)
            for c0 in range(0, nc_, gsz):
                g = min(gsz, nc_ - c0)
                sap = srcp.rearrange("(c p) f -> p c f", p=128)[:, c0:c0 + g, :]
                dap = dstp.rearrange("(c p) f -> p c f", p=128)[:, c0:c0 + g, :]
                stepsB.append(mk_cast(sap, dap, g * cols))

        def mk_kn(ib):
            def run():
                on("sync", lambda e: e.dma_start(
                    out=F2v[:, :, :], in_=knTv[:, :, ib * QT:(ib + 1) * QT]),
                    waits=[(vE, bF["v"]), (sE, bF["s"]), (p0S[1], bF["st"])],
                    inc=p0L[1], inc_by=16)
                ld = (p0L[1], p0L[1].n)
                on("scalar", lambda e: e.activation(
                    p0sv[1][:, :, :], F2v[:, :, :], AF.Square),
                    waits=[ld], inc=sE)
                sq = sE.n

                def knmm(e):
                    last = None
                    for c in range(DC):
                        last = e.matmul(B4[0][0:1, :], ones_col[:, :],
                                        p0sv[1][:, c, :],
                                        start=(c == 0), stop=(c == DC - 1))
                    return last
                on("tensor", knmm, waits=[(sE, sq), (gE, GINIT)], inc=tE)
                on("vector", lambda e: e.tensor_scalar(
                    r_t1[:, :], B4[0][0:1, :], EPS24, None, ALU.max),
                    waits=[(tE, tE.n)], inc=vE)
                on("scalar", lambda e: e.activation(r_t2[:, :], r_t1[:, :],
                                                    AF.Ln),
                   waits=[(vE, vE.n)], inc=sE)
                on("scalar", lambda e: e.activation(r_rstd[:, :], r_t2[:, :],
                                                    AF.Exp, scale=-0.5), inc=sE)
                on("tensor", lambda e: e.matmul(
                    B4[1][:, :], ones_rowD[0:1, :], r_rstd[0:1, :],
                    start=True, stop=True), waits=[(sE, sE.n)], inc=tE)
                bcv = tE.n

                def knscale(e):
                    last = None
                    for c in range(DC):
                        last = e.tensor_tensor(p0sv[1][:, c, :], F2v[:, c, :],
                                               B4[1][:, :], ALU.mult)
                    return last
                on("vector", knscale, waits=[(tE, bcv)], inc=vE)
                bF["v"] = vE.n
                bF["s"] = sE.n
                on("sync", lambda e: e.dma_start(
                    out=knbfv[:, :, ib * QT:(ib + 1) * QT],
                    in_=p0sv[1][:, :, :]),
                    waits=[(vE, vE.n)], inc=p0S[1], inc_by=16)
                bF["st"] = p0S[1].n
            return run

        for ib in range(DC):
            stepsB.append(mk_kn(ib))

        def mk_gamma(srcW, c0, g, cols, first, last, gcol):
            w = g * cols
            sap = srcW.rearrange("(c p) f -> p c f", p=128)[:, c0:c0 + g, :]

            def run():
                sl = a_slot["i"]; a_slot["i"] ^= 1
                aF = aS[sl]; fb = aFbuf[sl]
                stg = p0s[a_stg[sl]]; sg = a_stg[sl]
                on("scalar", lambda e: e.dma_start(out=fb[:, 0:w], in_=sap),
                    waits=[(vE, aF["v"]), (sE, aF["s"]), (p0S[sg], aF["st"])],
                    inc=p0L[sg], inc_by=16)
                ld = (p0L[sg], p0L[sg].n)
                on("scalar", lambda e: e.activation(
                    stg[:, 0:w], fb[:, 0:w], AF.Abs),
                    waits=[ld, (tE, aF["t"])], inc=sE)
                aF["s"] = sE.n

                def gmm(e):
                    last_i = None
                    nj = w // 128
                    for j in range(nj):
                        last_i = e.matmul(
                            gcol, stg[:, j * 128:(j + 1) * 128],
                            ones_col[:, :],
                            start=(first and j == 0),
                            stop=(last and j == nj - 1))
                    return last_i
                on("tensor", gmm, waits=[(sE, aF["s"]), (gE, GINIT)], inc=tE)
                aF["t"] = tE.n
            return run

        def mk_gfin(wi, rows, cols, gcol):
            rg0 = r_g[0:1, 2 * wi:2 * wi + 1]
            rg1 = r_g[0:1, 2 * wi + 1:2 * wi + 2]
            nelem = float(rows * cols)

            def run():
                on("vector", lambda e: e.tensor_copy(c_acc1[:, :], gcol),
                   waits=[(tE, max(aS[0]["t"], aS[1]["t"]))], inc=vE)
                on("tensor", lambda e: e.matmul(
                    PS4[0:1, 1024:1025], c_acc1[:, :], ones_colf[:, :],
                    start=True, stop=True), waits=[(vE, vE.n)], inc=tE)
                on("vector", lambda e: e.tensor_scalar(
                    rg0, PS4[0:1, 1024:1025], 1.0 / nelem, EPS5,
                    ALU.mult, ALU.add), waits=[(tE, tE.n)], inc=vE)
                on("vector", lambda e: e.reciprocal(rg1, rg0),
                   waits=[(vE, vE.n)], inc=vE)
                on("tensor", lambda e: e.matmul(
                    PS4[:, 1536:1537], ones_rowD[0:1, :], rg1,
                    start=True, stop=True), waits=[(vE, vE.n)], inc=tE)
                on("vector", lambda e: e.tensor_copy(
                    p_gsc[:, wi:wi + 1], PS4[:, 1536:1537]),
                   waits=[(tE, tE.n)], inc=vE)
                aS[0]["v"] = vE.n
                aS[1]["v"] = vE.n
            return run

        def mk_quant(srcW, dstW, c0, g, cols, wi):
            w = g * cols
            sap = srcW.rearrange("(c p) f -> p c f", p=128)[:, c0:c0 + g, :]
            dap = dstW.rearrange("(c p) f -> p c f", p=128)[:, c0:c0 + g, :]

            def run():
                sl = a_slot["i"]; a_slot["i"] ^= 1
                aF = aS[sl]; fb = aFbuf[sl]
                stg = p0s[a_stg[sl]]; sg = a_stg[sl]
                on("scalar", lambda e: e.dma_start(out=fb[:, 0:w], in_=sap),
                    waits=[(vE, aF["v"]), (sE, aF["s"]), (p0S[sg], aF["st"])],
                    inc=p0L[sg], inc_by=16)
                ld = (p0L[sg], p0L[sg].n)
                on("vector", lambda e: e.tensor_scalar(
                    fb[:, 0:w], fb[:, 0:w],
                    p_gsc[:, wi:wi + 1], MAGIC, ALU.mult, ALU.add),
                    waits=[ld], inc=vE)
                on("vector", lambda e: e.tensor_scalar(
                    fb[:, 0:w], fb[:, 0:w],
                    MAGIC, 1.0, ALU.subtract, ALU.min), inc=vE)
                on("vector", lambda e: e.tensor_scalar(
                    stg[:, 0:w], fb[:, 0:w],
                    -1.0, None, ALU.max), inc=vE)
                aF["v"] = vE.n
                on("scalar", lambda e: e.dma_start(out=dap, in_=stg[:, 0:w]),
                    waits=[(vE, vE.n)], inc=p0S[sg], inc_by=16)
                aF["st"] = p0S[sg].n
            return run

        for wi, (srcW, dstW, rows, cols) in enumerate(
                [(d_w1T, s_wq1, D, DFF), (d_w2T, s_wq2, DFF, D)]):
            nc_ = rows // 128
            gsz = max(1, 4096 // cols)
            units = list(range(0, nc_, gsz))
            gcol = PS4[:, 0:1] if wi == 0 else PS4[:, 512:513]
            for ui, c0 in enumerate(units):
                g = min(gsz, nc_ - c0)
                stepsA.append(mk_gamma(srcW, c0, g, cols,
                                       ui == 0, ui == len(units) - 1, gcol))
            stepsA.append(mk_gfin(wi, rows, cols, gcol))
            for c0 in units:
                g = min(gsz, nc_ - c0)
                stepsA.append(mk_quant(srcW, dstW, c0, g, cols, wi))

        ia = ib_ = 0
        while ia < len(stepsA) or ib_ < len(stepsB):
            if ia < len(stepsA):
                stepsA[ia](); ia += 1
            if ib_ < len(stepsB):
                stepsB[ib_](); ib_ += 1

        P0B_DONE = [(p0S[1], p0S[1].n)]
        P0A_ST = [(p0S[0], p0S[0].n), (p0S[2], p0S[2].n)]
        P0DONE = [(p0S[0], p0S[0].n), (p0S[1], p0S[1].n), (p0S[2], p0S[2].n)]

        def emit_ln(qt, w_prev=None, zout=None):
            if zout is None:
                zout = zb
            sl_ = slice(qt * QT, (qt + 1) * QT)
            wts = [(tE, tE.n), (sE, sE.n)] if w_prev is None else w_prev
            on("vector", lambda e: e.tensor_copy(xbf[:, :, :], xT[:, :, sl_]),
               waits=wts + [(vE, vE.n)], inc=vE)
            on("scalar", lambda e: e.activation(xsq[:, :, :], xT[:, :, sl_],
                                                AF.Square),
               waits=wts + [(vE, vE.n)], inc=sE)
            vcast, ssq = vE.n, sE.n

            def statmm(e):
                last = None
                for c in range(DC):
                    last = e.matmul(B4[0][0:1, :], ones_col[:, :], xbf[:, c, :],
                                    start=(c == 0), stop=(c == DC - 1))
                for c in range(DC):
                    last = e.matmul(B4[1][0:1, :], ones_col[:, :], xsq[:, c, :],
                                    start=(c == 0), stop=(c == DC - 1))
                return last
            on("tensor", statmm, waits=[(vE, vcast), (sE, ssq)], inc=tE)
            mmv = tE.n
            on("vector", lambda e: e.tensor_scalar(
                r_mu[:, :], B4[0][0:1, :], INV1024, None, ALU.mult),
                waits=[(tE, mmv)], inc=vE)
            on("vector", lambda e: e.scalar_tensor_tensor(
                r_t1[:, :], r_mu[:, :], -1.0, r_mu[:, :], ALU.mult, ALU.mult),
                inc=vE)
            on("vector", lambda e: e.scalar_tensor_tensor(
                r_t2[:, :], B4[1][0:1, :], INV1024, r_t1[:, :], ALU.mult, ALU.add),
                inc=vE)
            on("scalar", lambda e: e.activation(r_t1[:, :], r_t2[:, :], AF.Ln,
                                                bias=c_eps5[:, :]),
               waits=[(vE, vE.n)], inc=sE)
            on("scalar", lambda e: e.activation(r_rstd[:, :], r_t1[:, :], AF.Exp,
                                                scale=-0.5), inc=sE)
            srstd = sE.n
            on("vector", lambda e: e.scalar_tensor_tensor(
                r_nmr[:, :], r_mu[:, :], -1.0, r_rstd[:, :], ALU.mult, ALU.mult),
                waits=[(sE, srstd)], inc=vE)
            vnmr = vE.n
            on("tensor", lambda e: e.matmul(B4[2][:, :], ones_rowD[0:1, :],
                                            r_rstd[0:1, :], start=True, stop=True),
               waits=[(sE, srstd)], inc=tE)
            on("tensor", lambda e: e.matmul(B4[3][:, :], ones_rowD[0:1, :],
                                            r_nmr[0:1, :], start=True, stop=True),
               waits=[(vE, vnmr)], inc=tE)
            bcv = tE.n

            def applyA(e, sl_=sl_):
                last = None
                for c in range(DC):
                    last = e.tensor_tensor(TMPv[:, c, :], xT[:, c, sl_],
                                           B4[2][:, :], ALU.mult)
                return last

            def applyB(e, zout=zout):
                last = None
                for c in range(DC):
                    last = e.tensor_tensor(zout[:, c, :], TMPv[:, c, :],
                                           B4[3][:, :], ALU.add)
                return last
            on("vector", applyA, waits=[(tE, bcv)], inc=vE)
            on("vector", applyB, inc=vE)
            return vE.n

        def emit_l2recip(row_ap, wait_t):
            on("vector", lambda e: e.tensor_scalar(
                r_t1[:, :], row_ap, EPS24, None, ALU.max),
                waits=[(tE, wait_t)], inc=vE)
            on("scalar", lambda e: e.activation(r_t2[:, :], r_t1[:, :], AF.Ln),
               waits=[(vE, vE.n)], inc=sE)
            on("scalar", lambda e: e.activation(r_rstd[:, :], r_t2[:, :], AF.Exp,
                                                scale=-0.5), inc=sE)
            return sE.n

        tapsem2 = Sem(E(nc.semaphore("tapsem2"))) if TAPS else None

        def tap_sbuf(dst, src_ap):
            if not TAPS:
                return
            on("gpsimd", lambda e: e.dma_start(out=dst, in_=src_ap),
               waits=[(vE, vE.n), (sE, sE.n), (tE, tE.n)], inc=tapsem2,
               inc_by=16)
            on("vector", lambda e: e.engine_nop(),
               waits=[(tapsem2, tapsem2.n)], inc=vE)

        sb4 = [PS4[:, i * 512:(i + 1) * 512] for i in range(4)]
        evac_hist = []
        kn_ld_hist = []
        qw_hist = []
        def attna_pre(qt):
            wprev = None
            if qt == 0:
                wprev = [(ldXs, X_LOADED), (sE, sE.n)] + P0B_DONE
            zready = emit_ln(qt, wprev)
            if A_STAGE < 2:
                return (vE.n, vE.n)

            on("scalar", lambda e: e.activation(xsq[:, :, :], zb[:, :, :],
                                                AF.Square),
               waits=[(vE, zready)], inc=sE)
            ssq = sE.n

            def statmm2(e):
                last = None
                for c in range(DC):
                    last = e.matmul(B4[0][0:1, :], ones_col[:, :], zb[:, c, :],
                                    start=(c == 0), stop=(c == DC - 1))
                for c in range(DC):
                    last = e.matmul(B4[1][0:1, :], ones_col[:, :], xsq[:, c, :],
                                    start=(c == 0), stop=(c == DC - 1))
                return last
            on("tensor", statmm2, waits=[(vE, zready), (sE, ssq)], inc=tE)
            mmv = tE.n
            on("vector", lambda e: e.tensor_scalar(
                r_mu[:, :], B4[0][0:1, :], INV1024, None, ALU.mult),
                waits=[(tE, mmv)], inc=vE)
            on("vector", lambda e: e.scalar_tensor_tensor(
                r_t1[:, :], r_mu[:, :], -1.0, r_mu[:, :], ALU.mult, ALU.mult),
                inc=vE)
            on("vector", lambda e: e.scalar_tensor_tensor(
                r_t2[:, :], B4[1][0:1, :], INV1024, r_t1[:, :], ALU.mult, ALU.add),
                inc=vE)
            on("scalar", lambda e: e.activation(r_t1[:, :], r_t2[:, :], AF.Ln,
                                                bias=c_eps12[:, :]),
               waits=[(vE, vE.n)], inc=sE)
            on("scalar", lambda e: e.activation(r_rstd[:, :], r_t1[:, :], AF.Exp,
                                                scale=-0.5), inc=sE)
            srstd = sE.n
            on("vector", lambda e: e.scalar_tensor_tensor(
                r_nmr[:, :], r_mu[:, :], -1.0, r_rstd[:, :], ALU.mult, ALU.mult),
                waits=[(sE, srstd)], inc=vE)
            vnmr = vE.n
            on("tensor", lambda e: e.matmul(B4[2][:, :], ones_rowD[0:1, :],
                                            r_rstd[0:1, :], start=True, stop=True),
               waits=[(sE, srstd)], inc=tE)
            on("tensor", lambda e: e.matmul(B4[3][:, :], ones_rowD[0:1, :],
                                            r_nmr[0:1, :], start=True, stop=True),
               waits=[(vE, vnmr)], inc=tE)
            bcv = tE.n

            def apply2a(e):
                last = None
                for c in range(DC):
                    last = e.scalar_tensor_tensor(
                        TMPv[:, c, :], zb[:, c, :], p_lnw[:, c:c + 1],
                        B4[2][:, :], ALU.mult, ALU.mult)
                return last

            dterm = av(16 * K, 20 * K, QT)

            def apply2b2(e, dterm=dterm):
                last = None
                for c in range(DC):
                    last = e.tensor_scalar(
                        dterm[:, c, :], B4[3][:, :], p_lnw[:, c:c + 1],
                        p_lnb[:, c:c + 1], ALU.mult, ALU.add)
                return last

            def apply2c(e, dterm=dterm):
                last = None
                for c in range(DC):
                    last = e.tensor_tensor(z2b[:, c, :], TMPv[:, c, :],
                                           dterm[:, c, :], ALU.add)
                return last
            on("vector", apply2a, waits=[(tE, bcv)], inc=vE)
            on("vector", apply2b2, inc=vE)
            on("vector", apply2c, inc=vE)
            z2ready = vE.n

            on("scalar", lambda e: e.activation(xsq[:, :, :], z2b[:, :, :],
                                                AF.Square),
               waits=[(vE, z2ready)], inc=sE)
            ssq = sE.n

            def statmm3(e):
                last = None
                for c in range(DC):
                    last = e.matmul(B4[0][0:1, :], ones_col[:, :], xsq[:, c, :],
                                    start=(c == 0), stop=(c == DC - 1))
                return last
            on("tensor", statmm3, waits=[(sE, ssq)], inc=tE)
            sr = emit_l2recip(B4[0][0:1, :], tE.n)
            on("tensor", lambda e: e.matmul(B4[1][:, :], ones_rowD[0:1, :],
                                            r_rstd[0:1, :], start=True, stop=True),
               waits=[(sE, sr)], inc=tE)
            bcv = tE.n

            def applyQ(e):
                last = None
                for c in range(DC):
                    last = e.tensor_tensor(Q1[:, c, :], z2b[:, c, :],
                                           B4[1][:, :], ALU.mult)
                return last
            on("vector", applyQ, waits=[(tE, bcv)], inc=vE)
            return (z2ready, vE.n)

        XBv = z2b
        XSqv = zb
        pre_state = attna_pre(0) if (PHASES >= 2 and A_QT > 0) else None
        for qt in range((NQT if PHASES >= 2 else 0) if A_QT >= 4 else A_QT):
            sl_ = slice(qt * QT, (qt + 1) * QT)
            z2ready, qready = pre_state
            if A_STAGE < 3:
                if qt + 1 < NQT:
                    pre_state = attna_pre(qt + 1)
                continue

            F0h = [F0[:, 0:2048], F0[:, 2048:4096]]
            F2h = [F2[:, 0:2048], F2[:, 2048:4096]]
            sgh = [sgf[:, 0:2048], sgf[:, 2048:4096]]
            chain_vu = [0, 0]
            chain_vF0 = [z2ready, z2ready]
            for pair in range(4):
                pr_sg = []
                for gh in range(2):
                    g = 2 * pair + gh
                    kb = knb[g % 2]
                    wk = kn_ld_hist[-2][1] if len(kn_ld_hist) >= 2 else 0
                    on("sync", lambda e, g=g, kb=kb: e.dma_start(
                        out=kb[:, :, :],
                        in_=knbfv[:, :, g * QT:(g + 1) * QT]),
                        waits=[(tE, wk)] + P0B_DONE, inc=knLs[g % 2], inc_by=16)
                    kn_ld_hist.append((knLs[g % 2].n, tE.n))
                    for it4 in range(4):
                        it = g * 4 + it4
                        slot = it % 4
                        bank = sb4[slot]
                        wv_, ws_ = evac_hist[-4] if len(evac_hist) >= 4 else (0, 0)

                        def smm(e, bank=bank, it4=it4, kb=kb):
                            last = None
                            for c in range(DC):
                                last = e.matmul(
                                    bank, kb[:, c, it4 * 128:(it4 + 1) * 128],
                                    Q1[:, c, :],
                                    start=(c == 0), stop=(c == DC - 1))
                            return last
                        on("tensor", smm,
                           waits=[(knLs[g % 2], kn_ld_hist[-1][0]),
                                  (vE, qready), (sE, ws_)], inc=tE)
                        smv = tE.n
                        on("scalar", lambda e, bank=bank, gh=gh, it4=it4:
                            e.activation(sgb[:, gh * 4 + it4, :], bank, AF.Sign),
                            waits=[(tE, smv), (vE, chain_vu[gh])], inc=sE)
                        on("scalar", lambda e, bank=bank, gh=gh, it4=it4:
                            e.activation(F0v[:, gh * 4 + it4, :], bank, AF.Abs),
                            waits=[(vE, chain_vF0[gh])], inc=sE)
                        evac_hist.append((vE.n, sE.n))
                    kn_ld_hist[-1] = (kn_ld_hist[-1][0], tE.n)
                    pr_sg.append(sE.n)
                st = {}
                for gh in range(2):
                    on("vector", lambda e, gh=gh: e.tensor_scalar(
                        F0h[gh], F0h[gh], CLIP, None, ALU.min),
                        waits=[(sE, pr_sg[gh])], inc=vE)
                for gh in range(2):
                    on("vector", lambda e, gh=gh: e.scalar_tensor_tensor(
                        F2h[gh], F0h[gh], -1.0, F0h[gh], ALU.mult, ALU.mult),
                        inc=vE)
                    st[("na", gh)] = vE.n
                for gh in range(2):
                    on("scalar", lambda e, gh=gh: e.activation(
                        F2h[gh], F2h[gh], AF.Ln, bias=1.0),
                        waits=[(vE, st[("na", gh)])], inc=sE)
                for gh in range(2):
                    on("scalar", lambda e, gh=gh: e.activation(
                        F2h[gh], F2h[gh], AF.Exp, scale=-0.5), inc=sE)
                    st[("rc", gh)] = sE.n
                for gh in range(2):
                    on("vector", lambda e, gh=gh: e.tensor_scalar(
                        F0h[gh], F0h[gh], -1.0, 1.0, ALU.mult, ALU.add),
                        waits=[(sE, st[("rc", gh)])], inc=vE)
                for gh in range(2):
                    on("vector", lambda e, gh=gh: e.tensor_tensor(
                        F0h[gh], F0h[gh], F2h[gh], ALU.mult), inc=vE)
                    st[("w", gh)] = vE.n
                for gh in range(2):
                    on("scalar", lambda e, gh=gh: e.activation(
                        F2h[gh], F0h[gh], AF.Arctan),
                        waits=[(vE, st[("w", gh)])], inc=sE)
                    st[("at", gh)] = sE.n
                for gh in range(2):
                    on("vector", lambda e, gh=gh: e.tensor_scalar(
                        F0h[gh], F2h[gh], -2.0 / PI, 0.5, ALU.mult, ALU.add),
                        waits=[(sE, st[("at", gh)])], inc=vE)
                for gh in range(2):
                    on("vector", lambda e, gh=gh: e.tensor_tensor(
                        F2h[gh], F0h[gh], sgh[gh], ALU.mult), inc=vE)
                    st[("u", gh)] = vE.n
                    chain_vu[gh] = vE.n
                for gh in range(2):
                    on("scalar", lambda e, gh=gh: e.activation(
                        F0h[gh], F2h[gh], AF.Square, bias=c_half[:, :]),
                        waits=[(vE, st[("u", gh)])], inc=sE)
                for gh in range(2):
                    on("scalar", lambda e, gh=gh: e.activation(
                        F0h[gh], F0h[gh], AF.Square), inc=sE)
                for gh in range(2):
                    on("scalar", lambda e, gh=gh: e.activation(
                        F0h[gh], F0h[gh], AF.Square), inc=sE)
                    st[("t8", gh)] = sE.n
                for gh in range(2):
                    g = 2 * pair + gh
                    wb = (P0A_ST if (qt == 0 and pair == 0 and gh == 0) else [])
                    on("vector", lambda e, gh=gh, g=g: e.scalar_tensor_tensor(
                        Bbuf[:, 4 * g:4 * g + 4, :].rearrange("p a n -> p (a n)"),
                        F2h[gh], 0.5, F0h[gh], ALU.add, ALU.mult),
                        waits=[(sE, st[("t8", gh)])] + wb, inc=vE)
                    chain_vF0[gh] = vE.n
            b_ready = vE.n
            s_done = tE.n
            if qt == 0 and TAPS:
                tap_sbuf(d_tapB[:, :], Bbuf.rearrange("p a n -> p (a n)"))

            if qt + 1 < NQT and A_QT >= 4 and PHASES >= 2:
                pre_state = attna_pre(qt + 1)
            if A_STAGE < 4:
                continue

            half_free = (vE.n, sE.n)
            for half in range(2):
                for it in range(NKT):
                    qb = qwb[it % 2]
                    wq = qw_hist[-2] if len(qw_hist) >= 2 else 0
                    qwf = [(tE, wq)] if (it >= 2 or half > 0) else \
                        [(tE, wq), (vE, qready)]
                    on("sync", lambda e, it=it, half=half, qb=qb: e.dma_start(
                        out=qb[:, :],
                        in_=s_qwbf[it * 128:(it + 1) * 128,
                                   half * 512:(half + 1) * 512]),
                        waits=qwf + P0B_DONE, inc=qwLs[it % 2], inc_by=16)
                    ld = (qwLs[it % 2], qwLs[it % 2].n)

                    def xmm(e, it=it, qb=qb):
                        last = None
                        for c in range(4):
                            last = e.matmul(sb4[c], qb[:, c * 128:(c + 1) * 128],
                                            Bbuf[:, it, :],
                                            start=(it == 0), stop=(it == NKT - 1))
                        return last
                    on("tensor", xmm,
                       waits=[ld, (vE, max(b_ready, half_free[0])),
                              (sE, half_free[1])], inc=tE)
                    qw_hist.append(tE.n)
                xdone = tE.n
                on("scalar", lambda e, half=half: e.activation(
                    XSqv[:, 4 * half:4 * half + 4, :].rearrange("p a n -> p (a n)"),
                    PS4[:, :], AF.Square), waits=[(tE, xdone)], inc=sE)
                on("vector", lambda e, half=half: e.tensor_copy(
                    XBv[:, 4 * half:4 * half + 4, :].rearrange("p a n -> p (a n)"),
                    PS4[:, :]), waits=[(tE, xdone), (sE, sE.n)], inc=vE)
                half_free = (vE.n, sE.n)

            if qt == 0 and TAPS:
                tap_sbuf(d_tapX[:, :], XBv.rearrange("p a n -> p (a n)"))
            if A_STAGE < 5:
                continue

            def statmmX(e):
                last = None
                for c in range(DC):
                    last = e.matmul(B4[0][0:1, :], ones_col[:, :], XSqv[:, c, :],
                                    start=(c == 0), stop=(c == DC - 1))
                return last
            on("tensor", statmmX,
               waits=[(sE, half_free[1]), (vE, half_free[0])], inc=tE)
            on("vector", lambda e: e.tensor_scalar(
                rx0[:, :], B4[0][0:1, :], EPS24, None, ALU.max),
                waits=[(tE, tE.n)], inc=vE)
            on("scalar", lambda e: e.activation(rx1[:, :], rx0[:, :], AF.Ln),
               waits=[(vE, vE.n)], inc=sE)
            on("scalar", lambda e: e.activation(rx0[:, :], rx1[:, :], AF.Exp,
                                                scale=-0.5), inc=sE)
            sr = sE.n
            on("tensor", lambda e: e.matmul(B4[1][:, :], ones_rowD[0:1, :],
                                            rx0[0:1, :], start=True, stop=True),
               waits=[(sE, sr)], inc=tE)
            bcv = tE.n

            def applyresA(e):
                last = None
                for c in range(DC):
                    last = e.tensor_tensor(TMPv[:, c, :], XBv[:, c, :],
                                           B4[1][:, :], ALU.mult)
                return last

            def applyresB(e, sl_=sl_):
                last = None
                for c in range(DC):
                    last = e.scalar_tensor_tensor(
                        xT[:, c, sl_], xT[:, c, sl_], p_aab[:, c:c + 1],
                        TMPv[:, c, :], ALU.add, ALU.add)
                return last
            on("vector", applyresA, waits=[(tE, bcv)], inc=vE)
            on("vector", applyresB, inc=vE)

        tapsem = Sem(E(nc.semaphore("tapsem"))) if TAPS else None

        def emit_tap(i):
            if not (TAPS and (TAPS & (1 << i))):
                return
            light = bool(TAPS & 8)
            w = [(vE, vE.n)] if light else [(vE, vE.n), (tE, tE.n), (sE, sE.n)]
            for c in range(DC):
                on("gpsimd", lambda e, c=c, i=i: e.dma_start(
                    out=d_taps[i][c * 128:(c + 1) * 128, :], in_=xT[:, c, :]),
                    waits=w, inc=tapsem, inc_by=16)
            if not light:
                on("vector", lambda e: e.engine_nop(),
                   waits=[(tapsem, tapsem.n)], inc=vE)

        emit_tap(0)

        wq1v = s_wq1.rearrange("(c p) f -> p c f", p=128)
        wq2v = s_wq2.rearrange("(c p) f -> p c f", p=128)

        def emit_ffn():
            ev_free = {0: 0, 1: 0}
            init_tE, init_vE = tE.n, vE.n
            relu_hist = []
            w1_hist = []
            w2_hist = []
            ybufs = [zb, z2b]
            pend = emit_ln(0, zout=ybufs[0])
            for qt in range(NQT):
                sl_ = slice(qt * QT, (qt + 1) * QT)
                zready = pend
                yb = ybufs[qt % 2]
                for ft2 in range(NKT // 2):
                    buf = w1s[ft2 % 2]
                    wk = w1_hist[-2] if len(w1_hist) >= 2 else init_tE
                    on("sync", lambda e, ft2=ft2, buf=buf: e.dma_start(
                        out=buf[:, :, :],
                        in_=wq1v[:, :, ft2 * 256:(ft2 + 1) * 256]),
                        waits=[(tE, wk), (vE, init_vE)] + P0DONE,
                        inc=w1Ls[ft2 % 2], inc_by=16)
                    wld = (w1Ls[ft2 % 2], w1Ls[ft2 % 2].n)
                    for k in range(2):
                        ft = ft2 * 2 + k
                        bank = sb4[ft % 4]
                        rw = relu_hist[-4] if len(relu_hist) >= 4 else 0

                        def m1(e, bank=bank, buf=buf, k=k, yb=yb):
                            last = None
                            for c in range(DC):
                                last = e.matmul(
                                    bank, buf[:, c, k * 128:(k + 1) * 128],
                                    yb[:, c, :], start=(c == 0),
                                    stop=(c == DC - 1))
                            return last
                        on("tensor", m1,
                           waits=[wld, (vE, zready), (sE, rw)], inc=tE)
                        m1v = tE.n
                        on("scalar", lambda e, bank=bank, ft=ft: e.activation(
                            hbuf[:, ft, :], bank, AF.Relu,
                            bias=p_b1[:, ft:ft + 1]), waits=[(tE, m1v)], inc=sE)
                        relu_hist.append(sE.n)
                    w1_hist.append(tE.n)
                hready = sE.n
                if qt + 1 < NQT:
                    pend = emit_ln(qt + 1, zout=ybufs[(qt + 1) % 2])
                for dd in range(DC):
                    buf = w2s[dd % 2]
                    wk = w2_hist[-2] if len(w2_hist) >= 2 else init_tE
                    on("sync", lambda e, dd=dd, buf=buf: e.dma_start(
                        out=buf[:, :, :],
                        in_=wq2v[:, :, dd * 128:(dd + 1) * 128]),
                        waits=[(tE, wk), (vE, init_vE)] + P0DONE,
                        inc=w2Ls[dd % 2], inc_by=16)
                    wld = (w2Ls[dd % 2], w2Ls[dd % 2].n)
                    bank = B4[dd % 2][:, :]

                    def m2(e, bank=bank, buf=buf):
                        last = None
                        for fc in range(NKT):
                            last = e.matmul(bank, buf[:, fc, :], hbuf[:, fc, :],
                                            start=(fc == 0), stop=(fc == NKT - 1))
                        return last
                    on("tensor", m2,
                       waits=[wld, (sE, hready), (vE, ev_free[dd % 2])],
                       inc=tE)
                    m2v = tE.n
                    on("vector", lambda e, bank=bank, dd=dd, sl_=sl_:
                        e.scalar_tensor_tensor(
                            xT[:, dd, sl_], xT[:, dd, sl_], p_b2[:, dd:dd + 1],
                            bank, ALU.add, ALU.add),
                        waits=[(tE, m2v)], inc=vE)
                    ev_free[dd % 2] = vE.n
                    w2_hist.append(tE.n)

        if PHASES >= 3:
            emit_ffn()
        emit_tap(1)

        wqkvbv = s_wqkvbf.rearrange("(c p) f -> p c f", p=128)
        on("sync", lambda e: e.dma_start(
            out=wv[:, :, :], in_=wqkvbv[:, :, 2048:3072]),
            waits=[(tE, tE.n)] + P0DONE, inc=wvLs, inc_by=16)
        WV_ld = (wvLs, wvLs.n)

        qk_hist = []
        kv_dma = 0
        ATTNB_vE0 = vE.n
        for qt in range(NQT if PHASES >= 4 else 0):
            sl_ = slice(qt * QT, (qt + 1) * QT)
            zready = emit_ln(qt)
            ev_qk = 0
            for cc in range(16):
                buf = wqs[cc % 2]
                wk = qk_hist[-2] if len(qk_hist) >= 2 else 0
                on("sync", lambda e, cc=cc, buf=buf: e.dma_start(
                    out=buf[:, :, :], in_=wqkvbv[:, :, cc * 128:(cc + 1) * 128]),
                    waits=[(tE, wk), (vE, ATTNB_vE0)] + P0DONE,
                    inc=wqLs[cc % 2], inc_by=16)
                wld = (wqLs[cc % 2], wqLs[cc % 2].n)
                bank = B4[cc % 2][:, :]

                def qkm(e, bank=bank, buf=buf):
                    last = None
                    for c in range(DC):
                        last = e.matmul(bank, buf[:, c, :], zb[:, c, :],
                                        start=(c == 0), stop=(c == DC - 1))
                    return last
                on("tensor", qkm,
                   waits=[wld, (vE, max(zready, ev_qk))], inc=tE)
                qv = tE.n
                if cc < 8:
                    on("vector", lambda e, bank=bank, cc=cc, sl_=sl_:
                        e.tensor_scalar(
                            QTb[:, cc, sl_], bank, p_bqkv[:, cc:cc + 1], 0.125,
                            ALU.add, ALU.mult), waits=[(tE, qv)], inc=vE)
                else:
                    on("vector", lambda e, bank=bank, cc=cc: e.tensor_scalar(
                        kstg[:, :], bank, p_bqkv[:, cc:cc + 1], None, ALU.add),
                        waits=[(tE, qv), (ksSs, ksSs.n)], inc=vE)
                    on("scalar", lambda e, cc=cc, sl_=sl_: e.dma_start(
                        out=cc_inK.ap()[(cc - 8) * 128:(cc - 7) * 128, sl_],
                        in_=kstg[:, :]), waits=[(vE, vE.n)], inc=ksSs, inc_by=16)
                ev_qk = vE.n
                qk_hist.append(tE.n)
            vp_free = {0: 0, 1: 0}
            for tc in range(4):
                h0 = (tc % 2) * 1024
                vb0 = PS4[:, h0:h0 + 512]
                vb1 = PS4[:, h0 + 512:h0 + 1024]

                def vpre(e, vb0=vb0, vb1=vb1):
                    e.matmul(vb0, ones_rowD[0:1, :], p_bv[0:1, 0:512],
                             start=True, stop=False)
                    return e.matmul(vb1, ones_rowD[0:1, :],
                                    p_bv[0:1, 512:1024], start=True, stop=False)
                on("tensor", vpre,
                   waits=[WV_ld, (vE, vp_free[tc % 2])], inc=tE)

                def vmm(e, tc=tc, vb0=vb0, vb1=vb1):
                    last = None
                    for c in range(DC):
                        stat = zb[:, c, tc * 128:(tc + 1) * 128]
                        last = e.matmul(vb0, stat, wv[:, c, 0:512],
                                        start=False, stop=(c == DC - 1))
                        last = e.matmul(vb1, stat,
                                        wv[:, c, 512:1024],
                                        start=False, stop=(c == DC - 1))
                    return last
                on("tensor", vmm, inc=tE)
                vmv = tE.n
                vbuf = vplain[tc % 2]
                on("vector", lambda e, vbuf=vbuf, h0=h0: e.tensor_copy(
                    vbuf[:, :], PS4[:, h0:h0 + 1024]),
                    waits=[(tE, vmv), (vpSs[tc % 2], vpSs[tc % 2].n)], inc=vE)
                on("scalar", lambda e, vbuf=vbuf, qt=qt, tc=tc: e.dma_start(
                    out=cc_inV.ap()[qt * 512 + tc * 128:
                                    qt * 512 + (tc + 1) * 128, :],
                    in_=vbuf[:, :]), waits=[(vE, vE.n)],
                    inc=vpSs[tc % 2], inc_by=16)
                vp_free[tc % 2] = vE.n

        if PHASES >= 5:
            on("gpsimd", lambda e: e.collective_compute(
                "AllGather", ALU.bypass,
                replica_groups=[[0, 1], [2, 3], [4, 5], [6, 7]],
                ins=[cc_inK.ap().opt()], outs=[cc_outK.ap().opt()]),
                waits=[(ksSs, ksSs.n)], inc=ccK)
            on("gpsimd", lambda e: e.collective_compute(
                "AllGather", ALU.bypass,
                replica_groups=[[0, 1], [2, 3], [4, 5], [6, 7]],
                ins=[cc_inV.ap().opt()], outs=[cc_outV.ap().opt()]),
                waits=[(vpSs[0], vpSs[0].n), (vpSs[1], vpSs[1].n)], inc=ccV)
        GV = gE.n

        ccoutVv = cc_outV.ap().rearrange("(k p) d -> p k d", p=128)
        exp_hist = []
        av_hist = []
        pair_ev = 0
        pair_te = []
        ps4_guard = vE.n
        woutbv = s_woutbf.rearrange("(c p) f -> p c f", p=128)
        op_hist = []
        op_ev = {}
        QKV_TE = tE.n
        for qt in range(NQT if PHASES >= 5 else 0):
            for p in range(8):
                bslot = p % 2
                wk = pair_te[-2] if len(pair_te) >= 2 else QKV_TE
                kb = KTs[bslot]
                on("sync", lambda e, p=p, kb=kb: e.dma_start(
                    out=kb[:, 0:2048],
                    in_=cc_outK.ap()[p * 128:(p + 1) * 128, :]),
                    waits=[(ccK, 1), (tE, wk), (vpSs[0], vpSs[0].n),
                           (vpSs[1], vpSs[1].n), (ksSs, ksSs.n)],
                    inc=ktLs[bslot], inc_by=16)
                on("sync", lambda e, p=p, kb=kb: e.dma_start(
                    out=kb[:, 2048:4096],
                    in_=cc_outK.ap()[1024 + p * 128:1024 + (p + 1) * 128, :]),
                    inc=ktLs[bslot], inc_by=16)
                ktv = (ktLs[bslot], ktLs[bslot].n)
                vx = Vxs[bslot]
                h0, h1 = 2 * p, 2 * p + 1
                on("scalar", lambda e, h0=h0, vx=vx: e.dma_start(
                    out=vx[:, :, :], in_=ccoutVv[:, :, h0 * 64:(h0 + 2) * 64]),
                    waits=[(ccV, 1), (gE, GV), (tE, wk)],
                    inc=vxLs[bslot], inc_by=16)
                vxv = (vxLs[bslot], vxLs[bslot].n)

                pend_av = None
                for kc in range(NKT):
                    sp = PS4[:, (kc % 2) * 1024:(kc % 2 + 1) * 1024]
                    pb = Pbuf[kc % 2]
                    we = exp_hist[-2] if len(exp_hist) >= 2 else 0
                    wa = av_hist[-1] if len(av_hist) >= 1 else 0

                    def smm(e, sp=sp, kc=kc, kb=kb, qt=qt, p=p):
                        e.matmul(sp[:, 0:512],
                                 kb[0:64, kc * 128:(kc + 1) * 128],
                                 QTb[0:64, p, qt * QT:(qt + 1) * QT],
                                 start=True, stop=True, tile_position=(0, 0))
                        return e.matmul(
                            sp[:, 512:1024],
                            kb[64:128, kc * 128:(kc + 1) * 128],
                            QTb[64:128, p, qt * QT:(qt + 1) * QT],
                            start=True, stop=True, tile_position=(64, 0))
                    on("tensor", smm,
                       waits=[ktv, (sE, we), (vE, ps4_guard)], inc=tE)
                    smv = tE.n
                    on("scalar", lambda e, sp=sp, pb=pb: e.activation(
                        pb.rearrange("p h n -> p (h n)"), sp[:, :], AF.Exp),
                        waits=[(tE, smv), (tE, wa)], inc=sE)
                    exp_hist.append(sE.n)
                    sev = sE.n
                    if pend_av is not None:
                        on("tensor", pend_av[0], waits=pend_av[1], inc=tE)
                        av_hist.append(tE.n)

                    def avm(e, pb=pb, vx=vx, kc=kc):
                        e.matmul(B4[0][0:64, :], vx[:, kc, 0:64], pb[:, 0, :],
                                 start=(kc == 0), stop=(kc == NKT - 1),
                                 tile_position=(0, 0))
                        e.matmul(B4[1][64:128, :], vx[:, kc, 64:128],
                                 pb[:, 1, :],
                                 start=(kc == 0), stop=(kc == NKT - 1),
                                 tile_position=(0, 64))
                        e.matmul(B4[2][0:1, :], ones_col[:, :], pb[:, 0, :],
                                 start=(kc == 0), stop=(kc == NKT - 1),
                                 tile_position=(0, 0))
                        return e.matmul(
                            B4[3][32:33, :], ones_col[:, :], pb[:, 1, :],
                            start=(kc == 0), stop=(kc == NKT - 1),
                            tile_position=(0, 32))
                    pend_av = (avm, [vxv, (sE, sev), (vE, pair_ev)])
                on("tensor", pend_av[0], waits=pend_av[1], inc=tE)
                av_hist.append(tE.n)
                av_done = tE.n
                on("scalar", lambda e: e.activation(
                    r_t1[0:1, :], B4[2][0:1, :], AF.Ln),
                    waits=[(tE, av_done)], inc=sE)
                on("scalar", lambda e: e.activation(
                    r_rstd[0:1, :], r_t1[0:1, :], AF.Exp, scale=-1.0), inc=sE)
                on("scalar", lambda e: e.activation(
                    r_t1[0:1, :], B4[3][32:33, :], AF.Ln), inc=sE)
                on("scalar", lambda e: e.activation(
                    r_nmr[0:1, :], r_t1[0:1, :], AF.Exp, scale=-1.0), inc=sE)
                rv = sE.n

                def bcr(e):
                    e.matmul(B4[3][0:64, :], ones_rowD[0:1, 0:64],
                             r_rstd[0:1, :], start=True, stop=True,
                             tile_position=(0, 0))
                    return e.matmul(B4[2][64:128, :], ones_rowD[0:1, 0:64],
                                    r_nmr[0:1, :], start=True, stop=True,
                                    tile_position=(0, 64))
                on("tensor", bcr, waits=[(sE, rv), (vE, pair_ev)], inc=tE)
                bv_ = tE.n
                on("vector", lambda e: e.tensor_copy(
                    Obstg[0:64, :], B4[0][0:64, :]), waits=[(tE, bv_)], inc=vE)
                on("vector", lambda e: e.tensor_copy(
                    Obstg[64:128, :], B4[1][64:128, :]), inc=vE)
                on("vector", lambda e, p=p: e.tensor_tensor(
                    attno[0:64, p, :], Obstg[0:64, :], B4[3][0:64, :],
                    ALU.mult), inc=vE)
                on("vector", lambda e, p=p: e.tensor_tensor(
                    attno[64:128, p, :], Obstg[64:128, :], B4[2][64:128, :],
                    ALU.mult), inc=vE)
                pair_ev = vE.n
                pair_te.append(tE.n)
            sl_ = slice(qt * QT, (qt + 1) * QT)
            for dd in range(DC):
                buf = wobs[dd % 2]
                wk = op_hist[-2] if len(op_hist) >= 2 else QKV_TE
                on("sync", lambda e, dd=dd, buf=buf: e.dma_start(
                    out=buf[:, :, :], in_=woutbv[:, :, dd * 128:(dd + 1) * 128]),
                    waits=[(tE, wk)] + P0DONE, inc=opLs[dd % 2], inc_by=16)
                wld = (opLs[dd % 2], opLs[dd % 2].n)
                bank = PS4[:, (dd % 4) * 512:(dd % 4 + 1) * 512]

                def opm2(e, bank=bank, buf=buf, dd=dd):
                    last = e.matmul(bank, p_bout[0:1, dd * 128:(dd + 1) * 128],
                                    ones_rowN[0:1, :], start=True, stop=False)
                    for c in range(DC):
                        last = e.matmul(bank, buf[:, c, :], attno[:, c, :],
                                        start=False, stop=(c == DC - 1))
                    return last
                on("tensor", opm2,
                   waits=[wld, (vE, pair_ev), (vE, op_ev.get(dd % 4, 0))],
                   inc=tE)
                opv = tE.n
                on("vector", lambda e, bank=bank, dd=dd, sl_=sl_: e.tensor_tensor(
                    xT[:, dd, sl_], xT[:, dd, sl_], bank, ALU.add),
                    waits=[(tE, opv)], inc=vE)
                op_ev[dd % 4] = vE.n
                op_hist.append(tE.n)
            ps4_guard = vE.n

        emit_tap(2)
        if PHASES >= 6:
            emit_ffn()

        FIN = vE.n
        for c in range(DC):
            on("gpsimd", lambda e, c=c: e.dma_start(
                out=d_out[c * 128:(c + 1) * 128, :], in_=xT[:, c, :]),
                waits=[(vE, FIN)], inc=outQ, inc_by=16)
        on("gpsimd", lambda e: e.engine_nop(), waits=[(outQ, outQ.n)], inc=gE)

        with nc.Block() as block:
            @block.sync
            def _(e):
                for f in P.steps["sync"]:
                    f(e)

            @block.tensor
            def _(e):
                for f in P.steps["tensor"]:
                    f(e)

            @block.vector
            def _(e):
                for f in P.steps["vector"]:
                    f(e)

            @block.scalar
            def _(e):
                for f in P.steps["scalar"]:
                    f(e)

            @block.gpsimd
            def _(e):
                for f in P.steps["gpsimd"]:
                    f(e)

    return nc


_cache = {}


def kernel(**inputs):
    x = np.asarray(inputs["L0_attn_a_hidden_states"])
    kw = np.asarray(inputs["L0_attn_a_k_weight"])
    qw = np.asarray(inputs["L0_attn_a_q_weight"])
    lnw = np.asarray(inputs["L0_attn_a_ln_weight"])
    lnb = np.asarray(inputs["L0_attn_a_ln_bias"])
    aab = np.asarray(inputs["L0_attn_a_bias"])
    wqkv = np.asarray(inputs["L0_attn_b_w_qkv"])
    bqkv = np.asarray(inputs["L0_attn_b_b_qkv"])
    wout = np.asarray(inputs["L0_attn_b_w_out"])
    bout = np.asarray(inputs["L0_attn_b_b_out"])
    W1 = np.asarray(inputs["L0_ffn_W1"])
    b1 = np.asarray(inputs["L0_ffn_b1"])
    W2 = np.asarray(inputs["L0_ffn_W2"])
    b2 = np.asarray(inputs["L0_ffn_b2"])

    if "nc" not in _cache:
        _cache["nc"] = build_program()
    nc = _cache["nc"]

    com = {
        "knTw": np.ascontiguousarray(kw.T),
        "qw": np.ascontiguousarray(qw),
        "wqkv": np.ascontiguousarray(wqkv),
        "wout": np.ascontiguousarray(wout),
        "w1T": np.ascontiguousarray(W1.T),
        "w2T": np.ascontiguousarray(W2.T),
        "lnw_p": np.ascontiguousarray(lnw.reshape(DC, 128).T),
        "lnb_p": np.ascontiguousarray(lnb.reshape(DC, 128).T),
        "aab_p": np.ascontiguousarray(aab.reshape(DC, 128).T),
        "bqkv_p": np.ascontiguousarray(bqkv[:2048].reshape(16, 128).T),
        "bv_row": np.ascontiguousarray(bqkv[2048:].reshape(1, D)),
        "b1_p": np.ascontiguousarray(b1.reshape(NKT, 128).T),
        "b2_p": np.ascontiguousarray(b2.reshape(DC, 128).T),
        "bout_row": np.ascontiguousarray(bout.reshape(1, D)),
    }
    in_maps = []
    for core in range(8):
        b, h = core // 2, core % 2
        xs = np.ascontiguousarray(x[b, h * TPC:(h + 1) * TPC, :].T)
        m = dict(com)
        m["xT"] = xs
        in_maps.append(m)

    res = run_bass_kernel_spmd(nc, in_maps, core_ids=list(range(8)))
    out = np.empty((4, 4096, D), np.float32)
    for core in range(8):
        b, h = core // 2, core % 2
        out[b, h * TPC:(h + 1) * TPC, :] = res.results[core]["outT"].T
    return out
